# version 28
# baseline (speedup 1.0000x reference)
import contextlib
import numpy as np
import ml_dtypes

import concourse.bass as bass
import concourse.mybir as mybir
from concourse.bass_utils import run_bass_kernel_spmd

F32 = mybir.dt.float32
BF16 = mybir.dt.bfloat16
AF = mybir.ActivationFunctionType
ALU = mybir.AluOpType
AX = mybir.AxisListType

T, C = 8192, 2048
NT = T // 128
NS = 8
H, G, HD = 16, 4, 128
KV = G * HD
E, F = 8, 1408
NF = F // 128
FS = 5632
EPS_LN = 1e-5
EPS_QK = 1e-6
KILL = -30000.0
TRI = -60000.0
NJS = [8 * j + 8 for j in range(NS)]
JSOFF = [0]
for _n in NJS:
    JSOFF.append(JSOFF[-1] + _n)
NVAL = JSOFF[-1]


class Sems:

    def __init__(self, nc, es):
        self.nc, self.es = nc, es
        self.h = {}
        self.n = {}

    def get(self, name):
        if name not in self.h:
            self.h[name] = self.es.enter_context(self.nc.semaphore(name))
            self.n[name] = 0
        return self.h[name]

    def inc(self, inst, name, k=1):
        inst.then_inc(self.get(name), k)
        self.n[name] += k
        return self.n[name]

    def wait(self, eng, name, val):
        eng.wait_ge(self.get(name), val)

    def selfsync(self, eng, inst, name):
        v = self.inc(inst, name, 1)
        eng.wait_ge(self.h[name], v)

    def snap(self, *names):
        return {nm: self.n.get(nm, 0) for nm in names}


def glob_js(sl, js):
    return JSOFF[sl] + js + 1


def build_nc(stage=5, debug=(), shared_kv=True):
    nc = bass.Bass("TRN2", target_bir_lowering=False, num_devices=8)

    def din(name, shape, dt=F32):
        return nc.dram_tensor(name, list(shape), dt, kind="ExternalInput").ap()

    x_p1 = din("x_p1", [T // 2, C])
    x_own = din("x_own", [NS * 128, C])
    wq = din("wq", [C, C])
    bq = din("bq", [1, C])
    wkv = din("wkv", [C, KV * 2])
    bkv = din("bkv", [1, KV * 2])
    wo = din("wo", [128, 4 * 16 * 512])
    bo = din("bo", [1, C])
    alpha = din("alpha", [1, H])
    wr9 = din("wr9", [C, 9])
    wgu_d = din("wgu", [E + 4, NF, 128, 16 * 256])
    wdd_d = din("wdd", [E + 4, 2, 128, NF * 1024])
    cos_t = din("cos_t", [T // 2, 64])
    sin_t = din("sin_t", [T // 2, 64])
    cos_o = din("cos_o", [NS * 128, 64])
    sin_o = din("sin_o", [NS * 128, 64])
    identd = din("ident", [128, 128], BF16)
    onesr_d = din("ones_row", [1, 512], BF16)
    onesc_d = din("ones_col", [128, 3], F32)
    oneslh_d = din("ones_lhs", [1, 128], F32)
    trimask_d = din("trimask", [128, 512], BF16)
    valid_d = din("valid", [128, NVAL])

    out_own = nc.dram_tensor("out_own", [NS * 128, C], F32, kind="ExternalOutput").ap()

    dbg = {}

    def dout(name, shape, dt=F32):
        dbg[name] = nc.dram_tensor("dbg_" + name, list(shape), dt, kind="ExternalOutput").ap()
        return dbg[name]

    _asp = "Shared" if shared_kv else "Local"
    kv_k = nc.dram_tensor("kv_k", [G, 128, T], BF16, kind="Internal",
                          addr_space=_asp).ap()
    kv_v = nc.dram_tensor("kv_v", [T, KV], BF16, kind="Internal",
                          addr_space=_asp).ap()
    kv_k_loc = nc.dram_tensor("kv_k_loc", [G, 128, T // 2], BF16, kind="Internal").ap()
    kv_v_loc = nc.dram_tensor("kv_v_loc", [T // 2, KV], BF16, kind="Internal").ap()
    NEXP = E + 4

    eso = contextlib.ExitStack()
    with eso:
        S = Sems(nc, eso)
        sb = lambda n_, sh, dt: eso.enter_context(nc.sbuf_tensor(n_, list(sh), dt))
        sbr = lambda n_, sh, dt: eso.enter_context(nc.sbuf_tensor(n_, list(sh), dt, side="right"))

        ident = sb("identb", [128, 128], BF16)

        onesc = sb("onescb", [128, 3], F32)

        alf = sb("alfb", [1, H], F32)
        gneg = sb("gnegb", [1, G], F32)

        esa = contextlib.ExitStack()
        sba = lambda n_, sh, dt: esa.enter_context(nc.sbuf_tensor(n_, list(sh), dt, side="right"))
        trim = sba("trimb", [128, 512], BF16)
        bo_sb = sba("bo_sbb", [1, C], BF16)
        onesr = sba("onesrb", [1, 512], BF16)
        oneslh = sba("oneslhb", [1, 128], F32)
        alpha_bc = sba("alpha_bcb", [128, H], F32)
        negb_bc = sba("negb_bcb", [128, G], F32)
        tmp30 = sba("tmp30b", [128, G], F32)
        bias_tab = sba("bias_tabb", [128, G, NVAL], F32)

        eskv = contextlib.ExitStack()
        wkv_sb = eskv.enter_context(nc.sbuf_tensor("wkv_sb", [128, 16, KV * 2], BF16))
        bq_sb = eskv.enter_context(nc.sbuf_tensor("bq_sbb", [1, C], BF16))
        bkv_sb = eskv.enter_context(nc.sbuf_tensor("bkv_sbb", [1, KV * 2], BF16))
        coso = eskv.enter_context(nc.sbuf_tensor("coso", [128, NS, 64], BF16))
        sino = eskv.enter_context(nc.sbuf_tensor("sino", [128, NS, 64], BF16))

        with nc.sbuf_tensor("validb", [128, NVAL], F32, side="right") as valid, \
             nc.psum_tensor("p0b", [128, H + G], F32) as p0b, \
             nc.Block("p0", no_gpsimd_drain=True) as blk:
            @blk.gpsimd
            def _(g):
                for dst, src in ((ident, identd), (onesr, onesr_d), (onesc, onesc_d),
                                 (oneslh, oneslh_d), (trim, trimask_d), (valid, valid_d),
                                 (alf, alpha), (bq_sb, bq), (bkv_sb, bkv), (bo_sb, bo)):
                    S.inc(g.dma_start(out=dst[...], in_=src[...]), "d_c0", 16)

            @blk.vector
            def _(v):
                S.wait(v, "d_c0", S.n["d_c0"])
                for g_ in range(G):
                    inst = v.reduce_max(gneg[:, g_:g_ + 1], alf[:, 4 * g_:4 * g_ + 4],
                                        axis=AX.X, apply_absolute_value=True, negate=True)
                S.inc(inst, "v_gn")
                S.wait(v, "t_p0", 1)
                v.tensor_copy(alpha_bc[:, :], p0b[:, 0:H])
                inst = v.tensor_copy(negb_bc[:, :], p0b[:, H:H + G])
                S.selfsync(v, inst, "v_ss")
                inst = v.tensor_scalar(tmp30[:, :], negb_bc[:, :], -KILL, None, ALU.add)
                S.selfsync(v, inst, "v_ss")
                for g_ in range(G):
                    inst = v.tensor_scalar(bias_tab[:, g_, :], valid[:, :],
                                           tmp30[:, g_:g_ + 1], KILL, ALU.mult, ALU.add)
                S.inc(inst, "v_p0")

            @blk.tensor
            def _(t):
                S.wait(t, "v_gn", 1)
                t.matmul(p0b[:, 0:H], oneslh[:, :], alf[:, :], start=True, stop=True)
                inst = t.matmul(p0b[:, H:H + G], oneslh[:, :], gneg[:, :], start=True, stop=True)
                S.inc(inst, "t_p0")

        if stage >= 1:
            with contextlib.ExitStack() as p1:
                psb = lambda n_, sh, dt: p1.enter_context(nc.psum_tensor(n_, list(sh), dt))
                p1b_ = lambda n_, sh, dt: p1.enter_context(nc.sbuf_tensor(n_, list(sh), dt))
                cosf = p1b_("cosf", [128, NT // 2, 64], BF16)
                sinf = p1b_("sinf", [128, NT // 2, 64], BF16)
                xin = p1b_("xin", [128, 4, C], F32)
                xa = p1b_("xa", [128, 4, C], BF16)
                xaT = p1b_("xaT", [128, 4, 16, 128], BF16)
                stats = p1b_("stats", [128, 24], F32)
                mv = p1b_("mv", [128, 4, 2], F32)
                lnb = p1b_("lnb", [128, 4], F32)
                rstd = p1b_("rstd", [128, 4], F32)
                ksb = p1b_("ksb", [128, 2, KV], F32)
                junk = p1b_("junk", [128, 512], F32)
                vjunk = p1b_("vjunk", [128, G, 256], F32)
                ssb = p1b_("ssb", [128, 2, G], F32)
                sqb = p1b_("sqb", [128, 2, G], F32)
                reck = p1b_("reck", [128, 2, G], F32)
                khat = p1b_("khat", [128, KV], F32)
                kr = p1b_("kr", [128, 4, KV], BF16)
                kTs = p1b_("kTs", [128, 3, G, 4, 128], BF16)
                vbuf = p1b_("vbuf", [128, 8, KV], BF16)
                trp = (psb("tr0", [128, 8, 128], BF16), psb("tr1", [128, 8, 128], BF16))
                kvp2b = psb("kvp2b", [128, 2, 512], F32)
                ktrp = [psb("ktr%d" % i_, [128, G, 128], BF16) for i_ in range(4)]

                NTH = NT // 2
                with nc.Block("p1", no_gpsimd_drain=True) as blk:
                    @blk.gpsimd
                    def _(g):
                        S.inc(g.dma_start(out=wkv_sb[...],
                                          in_=wkv.rearrange("(ct p) n -> p ct n", p=128)), "d_w1", 16)
                        S.inc(g.dma_start(out=cosf[...],
                                          in_=cos_t.rearrange("(n p) d -> p n d", p=128)), "d_w1", 16)
                        S.inc(g.dma_start(out=sinf[...],
                                          in_=sin_t.rearrange("(n p) d -> p n d", p=128)), "d_w1", 16)
                        S.inc(g.dma_start(out=coso[...],
                                          in_=cos_o.rearrange("(n p) d -> p n d", p=128)), "d_w2", 16)
                        S.inc(g.dma_start(out=sino[...],
                                          in_=sin_o.rearrange("(n p) d -> p n d", p=128)), "d_w2", 16)
                        vvr4 = kv_v_loc.rearrange("(n p) d -> p n d", p=128)
                        for w in range(NTH):
                            if w % 4 == 3:
                                strip = w // 4
                                S.wait(g, "s_ke", w + 1)
                                for g_ in range(G):
                                    S.inc(g.dma_start(
                                        out=kv_k_loc[g_][:, 512 * strip:512 * (strip + 1)],
                                        in_=kTs[:, strip % 3, g_, :, :].rearrange(
                                            "p a b -> p (a b)")),
                                          "d_ko%d" % (strip % 4), 16)
                            if w % 2 == 1:
                                S.wait(g, "s_ev", 2 * w + 2)
                                S.inc(g.dma_start(
                                    out=vvr4[:, w - 1:w + 1, :],
                                    in_=vbuf[:, (w - 1) % 8:(w - 1) % 8 + 2, :]),
                                      "d_vo%d" % ((w // 2) % 4), 16)

                    @blk.sync
                    def _(sy):
                        xf_t = x_p1.rearrange("(n p) c -> p n c", p=128)
                        for i in range(NTH):
                            if i >= 4:
                                S.wait(sy, "v_xa", i - 3)
                            S.inc(sy.dma_start(out=xin[:, i % 4, 0:1024],
                                               in_=xf_t[:, i, 0:1024]),
                                  "d_x%d" % (i % 4), 16)
                            S.inc(sy.dma_start(out=xin[:, i % 4, 1024:2048],
                                               in_=xf_t[:, i, 1024:2048]),
                                  "d_x%d" % (i % 4), 16)

                    @blk.vector
                    def _(v):
                        NW = NTH // 4
                        S.wait(v, "d_w1", 48)
                        for w in range(NW + 1):
                            if w < NW:
                                for j4 in range(4):
                                    i = 4 * w + j4
                                    S.wait(v, "d_x%d" % (i % 4), 32 * (i // 4 + 1))
                                    for cch in range(4):
                                        inst = v.bn_stats(stats[:, 6 * cch:6 * cch + 6],
                                                          xin[:, i % 4,
                                                              512 * cch:512 * cch + 512])
                                    S.selfsync(v, inst, "v_ss")
                                    inst = v.bn_aggr(mv[:, j4, :], stats[:, :])
                                    S.wait(v, "v_mv", S.inc(inst, "v_mv"))
                            if w >= 1:
                                if w >= 2:
                                    S.wait(v, "t_kt", 4 * (w - 1))
                                for j4 in range(4):
                                    j = 4 * (w - 1) + j4
                                    pj = j % 2
                                    S.wait(v, "s_rec", j + 1)
                                    for g_ in range(G):
                                        inst = v.tensor_scalar(
                                            khat[:, HD * g_:HD * (g_ + 1)],
                                            ksb[:, pj, HD * g_:HD * (g_ + 1)],
                                            reck[:, pj, g_:g_ + 1], None, ALU.mult)
                                    S.selfsync(v, inst, "v_ss")
                                    for g_ in range(G):
                                        x1 = khat[:, HD * g_:HD * g_ + 64]
                                        x2_ = khat[:, HD * g_ + 64:HD * g_ + 128]
                                        ct_ = cosf[:, j, :]
                                        st_ = sinf[:, j, :]
                                        v.tensor_mul(vjunk[:, g_, 0:64], x1, ct_)
                                        v.tensor_mul(vjunk[:, g_, 64:128], x2_, st_)
                                        v.tensor_mul(vjunk[:, g_, 128:192], x1, st_)
                                        inst = v.tensor_mul(vjunk[:, g_, 192:256], x2_, ct_)
                                    S.selfsync(v, inst, "v_ss")
                                    for g_ in range(G):
                                        v.tensor_sub(kr[:, j4, HD * g_:HD * g_ + 64],
                                                     vjunk[:, g_, 0:64], vjunk[:, g_, 64:128])
                                        inst = v.tensor_add(
                                            kr[:, j4, HD * g_ + 64:HD * g_ + 128],
                                            vjunk[:, g_, 128:192], vjunk[:, g_, 192:256])
                                    S.inc(inst, "v_kr")
                            if w < NW:
                                for j4 in range(4):
                                    i = 4 * w + j4
                                    S.wait(v, "s_rs", i + 1)
                                    inst = v.tensor_scalar(xa[:, i % 4, :], xin[:, i % 4, :],
                                                           mv[:, j4, 0:1], rstd[:, j4:j4 + 1],
                                                           ALU.subtract, ALU.mult)
                                    S.inc(inst, "v_xa")
                                if w >= 1:
                                    S.wait(v, "t_kv", 8 * w)
                                for j4 in range(4):
                                    i = 4 * w + j4
                                    for r in range(2):
                                        R = 2 * i + r
                                        S.wait(v, "t_tr", R + 1)
                                        inst = v.tensor_copy(xaT[:, i % 4, 8 * r:8 * r + 8, :],
                                                             trp[R % 2][:, :, :])
                                        S.inc(inst, "v_xt")

                    @blk.scalar
                    def _(s):
                        NW = NTH // 4
                        for w in range(NW + 2):
                            if w >= 2:
                                for j4 in range(4):
                                    k2 = 4 * (w - 2) + j4
                                    S.wait(s, "t_kt", k2 + 1)
                                    if k2 >= 12 and k2 % 4 == 0:
                                        st_i = k2 // 4 - 3
                                        S.wait(s, "d_ko%d" % (st_i % 4), 64 * (st_i // 4 + 1))
                                    inst = s.copy(kTs[:, (k2 // 4) % 3, :, k2 % 4, :],
                                                  ktrp[k2 % 4][:, :, :])
                                    S.inc(inst, "s_ke")
                            if 1 <= w <= NW:
                                for j4 in range(4):
                                    j = 4 * (w - 1) + j4
                                    pj = j % 2
                                    S.wait(s, "t_kv", 2 * j + 1)
                                    if j >= 2:
                                        S.wait(s, "v_kr", j - 1)
                                    inst = s.copy(ksb[:, pj, :], kvp2b[:, 0, :])
                                    S.wait(s, "s_ev", S.inc(inst, "s_ev"))
                                    for g_ in range(G):
                                        inst = s.activation(
                                            junk[:, HD * g_:HD * (g_ + 1)],
                                            ksb[:, pj, HD * g_:HD * (g_ + 1)],
                                            AF.Square, accum_out=ssb[:, pj, g_:g_ + 1])
                                    S.selfsync(s, inst, "s_ss")
                                    inst = s.activation(sqb[:, pj, :], ssb[:, pj, :], AF.Ln,
                                                        bias=onesc[:, 2:3], scale=1.0)
                                    S.selfsync(s, inst, "s_ss")
                                    inst = s.activation(reck[:, pj, :], sqb[:, pj, :], AF.Exp,
                                                        bias=0.0, scale=-0.5)
                                    S.inc(inst, "s_rec")
                                    S.wait(s, "t_kv", 2 * j + 2)
                                    if j >= 8:
                                        pr = ((j - 8) // 2)
                                        S.wait(s, "d_vo%d" % (pr % 4), 16 * (pr // 4 + 1))
                                    inst = s.copy(vbuf[:, j % 8, :], kvp2b[:, 1, :])
                                    S.inc(inst, "s_ev")
                            if w < NW:
                                for j4 in range(4):
                                    i = 4 * w + j4
                                    S.wait(s, "v_mv", i + 1)
                                    inst = s.activation(lnb[:, j4:j4 + 1], mv[:, j4, 1:2], AF.Ln,
                                                        bias=onesc[:, 1:2], scale=1.0)
                                    S.selfsync(s, inst, "s_ss")
                                    inst = s.activation(rstd[:, j4:j4 + 1], lnb[:, j4:j4 + 1],
                                                        AF.Exp, bias=0.0, scale=-0.5)
                                    S.inc(inst, "s_rs")
                    @blk.tensor
                    def _(t):
                        NW = NTH // 4
                        S.wait(t, "d_w1", 48)
                        S.wait(t, "v_p0", 1)
                        for w in range(NW + 1):
                            if 1 <= w <= NW:
                                for j4 in range(4):
                                    j = 4 * (w - 1) + j4
                                    S.wait(t, "v_xt", 2 * (j + 1))
                                    for q in range(2):
                                        c = 2 * j + q
                                        if c >= 2:
                                            S.wait(t, "s_ev", c - 1)
                                        for ct in range(16):
                                            t.matmul(kvp2b[:, q, :], xaT[:, j % 4, ct, :],
                                                     wkv_sb[:, ct, 512 * q:512 * (q + 1)],
                                                     start=(ct == 0), stop=False)
                                        inst = t.matmul(kvp2b[:, q, :], onesr[0:1, 0:128],
                                                        bkv_sb[:, 512 * q:512 * (q + 1)],
                                                        start=False, stop=True)
                                        S.inc(inst, "t_kv")
                            if w < NW:
                                for j4 in range(4):
                                    i = 4 * w + j4
                                    S.wait(t, "v_xa", i + 1)
                                    for r in range(2):
                                        R = 2 * i + r
                                        if R >= 2:
                                            S.wait(t, "v_xt", R - 1)
                                        for c8 in range(8):
                                            inst = t.transpose(
                                                trp[R % 2][:, c8, :],
                                                xa[:, i % 4,
                                                   128 * (8 * r + c8):128 * (8 * r + c8 + 1)],
                                                ident[:, :])
                                        S.inc(inst, "t_tr")
                            if w >= 1:
                                for j4 in range(4):
                                    k2 = 4 * (w - 1) + j4
                                    S.wait(t, "v_kr", k2 + 1)
                                    if k2 >= 4:
                                        S.wait(t, "s_ke", k2 - 3)
                                    for g_ in range(G):
                                        inst = t.transpose(
                                            ktrp[k2 % 4][:, g_, :],
                                            kr[:, k2 % 4, HD * g_:HD * (g_ + 1)],
                                            ident[:, :])
                                    S.inc(inst, "t_kt")

        p1_fin = S.snap("s_ke", "s_ev", "v_xt",
                        "d_ko0", "d_ko1", "d_ko2", "d_ko3",
                        "d_vo0", "d_vo1", "d_vo2", "d_vo3",
                        "t_kv", "t_kt", "v_kr", "v_xa", "t_tr", "d_cp")

        esq = contextlib.ExitStack()
        sbq = lambda n_, sh, dt: esq.enter_context(nc.sbuf_tensor(n_, list(sh), dt, side="right"))
        qT = sbq("qTb", [128, H, NS, 128], BF16)
        kT_own = sbq("kT_ownb", [128, G, NS, 128], BF16)
        v_own = sbq("v_ownb", [128, NS, G, 128], BF16)

        if stage >= 2:
            with contextlib.ExitStack() as p2s:
                lsb = lambda n_, sh, dt: p2s.enter_context(nc.sbuf_tensor(n_, list(sh), dt))
                xaT_own = lsb("xaT_own", [128, 16, NS, 128], BF16)
                wqb4 = lsb("wqb4", [128, 2, 16, 512], BF16)
                xin2 = lsb("xin2", [128, 2, C], F32)
                xa2 = lsb("xa2", [128, 2, C], BF16)
                stats2 = lsb("stats2", [128, 24], F32)
                mv2 = lsb("mv2", [128, 2, 2], F32)
                lnb2 = lsb("lnb2", [128, 2], F32)
                rstd2 = lsb("rstd2", [128, 2], F32)
                ksb2 = lsb("ksb2", [128, 2, KV], F32)
                vjunk2 = lsb("vjunk2", [128, G, 256], F32)
                ssb2 = lsb("ssb2", [128, 2, G], F32)
                sqb2 = lsb("sqb2", [128, 2, G], F32)
                reck2 = lsb("reck2", [128, 2, G], F32)
                kr2 = lsb("kr2", [128, 2, KV], BF16)
                qsb4 = lsb("qsb4", [128, 2, 512], F32)
                qss4 = lsb("qss4", [128, 2, 4], F32)
                qln4 = lsb("qln4", [128, 2, 4], F32)
                qrec4 = lsb("qrec4", [128, 2, 4], F32)
                qrm = lsb("qrm", [128, 4], F32)
                qr4 = lsb("qr4", [128, 2, 512], BF16)

                with contextlib.ExitStack() as pA:
                    psbA = lambda n_, sh, dt: pA.enter_context(nc.psum_tensor(n_, list(sh), dt))
                    junk2 = pA.enter_context(nc.sbuf_tensor("junk2", [128, 512], F32))
                    trp2 = (psbA("tr20", [128, 8, 128], BF16), psbA("tr21", [128, 8, 128], BF16))
                    kvp2 = (psbA("kv20", [128, KV * 2], F32), psbA("kv21", [128, KV * 2], F32))
                    ktrp2 = (psbA("ktr20", [128, G, 128], BF16), psbA("ktr21", [128, G, 128], BF16))

                    with nc.Block("p1bA", no_gpsimd_drain=True) as blk:
                        @blk.gpsimd
                        def _(g):
                            S.wait(g, "s_ke", p1_fin["s_ke"])
                            S.wait(g, "s_ev", p1_fin["s_ev"])
                            S.wait(g, "t_kt", p1_fin["t_kt"])
                            for nm_ in ("d_ko0", "d_ko1", "d_ko2", "d_ko3",
                                        "d_vo2", "d_vo3"):
                                S.wait(g, nm_, p1_fin[nm_])
                            S.wait(g, "d_vo0", p1_fin["d_vo0"])
                            S.wait(g, "d_vo1", p1_fin["d_vo1"])

                        @blk.sync
                        def _(sy):
                            for nm_ in ("d_ko0", "d_ko1", "d_ko2", "d_ko3",
                                        "d_vo0", "d_vo1", "d_vo2", "d_vo3"):
                                S.wait(sy, nm_, p1_fin[nm_])
                            pid_ = sy.partition_id()
                            hh_ = pid_ % 2
                            S.inc(sy.dma_start(
                                out=kv_k.rearrange("g p (hh s) -> p g hh s", hh=2)[
                                    :, :, bass.ts(hh_, 1), :],
                                in_=kv_k_loc.rearrange("g p s -> p g s")), "d_cp", 16)
                            S.inc(sy.dma_start(
                                out=kv_v.rearrange("(hh r p) d -> p hh r d", hh=2, p=128)[
                                    :, bass.ts(hh_, 1), :, :],
                                in_=kv_v_loc.rearrange("(r p) d -> p r d", p=128)),
                                  "d_cp", 16)
                            xo_t = x_own.rearrange("(n p) c -> p n c", p=128)
                            for sl in range(NS):
                                if sl >= 2:
                                    S.wait(sy, "v_xa2", sl - 1)
                                S.inc(sy.dma_start(out=xin2[:, sl % 2, 0:1024],
                                                   in_=xo_t[:, sl, 0:1024]),
                                      "d_x2%d" % (sl % 2), 16)
                                S.inc(sy.dma_start(out=xin2[:, sl % 2, 1024:2048],
                                                   in_=xo_t[:, sl, 1024:2048]),
                                      "d_x2%d" % (sl % 2), 16)

                        @blk.vector
                        def _(v):
                            S.wait(v, "d_w2", 32)
                            for it in range(NS // 2 + 1):
                                if it < NS // 2:
                                    for j2 in range(2):
                                        sl = 2 * it + j2
                                        p = sl % 2
                                        S.wait(v, "d_x2%d" % p, 32 * (sl // 2 + 1))
                                        for cch in range(4):
                                            inst = v.bn_stats(
                                                stats2[:, 6 * cch:6 * cch + 6],
                                                xin2[:, p, 512 * cch:512 * cch + 512])
                                        S.selfsync(v, inst, "v_ss")
                                        inst = v.bn_aggr(mv2[:, j2, :], stats2[:, :])
                                        S.wait(v, "v_mv2", S.inc(inst, "v_mv2"))
                                for j2 in range(2):
                                    j = 2 * (it - 1) + j2
                                    if not 0 <= j < NS:
                                        continue
                                    pj = j % 2
                                    S.wait(v, "s_rec2", j + 1)
                                    for g_ in range(G):
                                        inst = v.tensor_scalar(ksb2[:, pj, HD * g_:HD * (g_ + 1)],
                                                               ksb2[:, pj, HD * g_:HD * (g_ + 1)],
                                                               reck2[:, pj, g_:g_ + 1], None, ALU.mult)
                                    S.selfsync(v, inst, "v_ss")
                                    if j >= 2:
                                        S.wait(v, "t_kt2", j - 1)
                                    for g_ in range(G):
                                        x1 = ksb2[:, pj, HD * g_:HD * g_ + 64]
                                        x2_ = ksb2[:, pj, HD * g_ + 64:HD * g_ + 128]
                                        ct_ = coso[:, j, :]
                                        st_ = sino[:, j, :]
                                        v.tensor_mul(vjunk2[:, g_, 0:64], x1, ct_)
                                        v.tensor_mul(vjunk2[:, g_, 64:128], x2_, st_)
                                        v.tensor_mul(vjunk2[:, g_, 128:192], x1, st_)
                                        inst = v.tensor_mul(vjunk2[:, g_, 192:256], x2_, ct_)
                                    S.selfsync(v, inst, "v_ss")
                                    for g_ in range(G):
                                        v.tensor_sub(kr2[:, pj, HD * g_:HD * g_ + 64],
                                                     vjunk2[:, g_, 0:64], vjunk2[:, g_, 64:128])
                                        inst = v.tensor_add(kr2[:, pj, HD * g_ + 64:HD * g_ + 128],
                                                            vjunk2[:, g_, 128:192],
                                                            vjunk2[:, g_, 192:256])
                                    S.inc(inst, "v_kr2")
                                if it < NS // 2:
                                    for j2 in range(2):
                                        sl = 2 * it + j2
                                        p = sl % 2
                                        S.wait(v, "s_rs2", sl + 1)
                                        if j2 == 0 and it >= 1:
                                            S.wait(v, "t_tr2", 4 * it)
                                        inst = v.tensor_scalar(xa2[:, p, :], xin2[:, p, :],
                                                               mv2[:, j2, 0:1],
                                                               rstd2[:, j2:j2 + 1],
                                                               ALU.subtract, ALU.mult)
                                        S.inc(inst, "v_xa2")
                                    for j2 in range(2):
                                        sl = 2 * it + j2
                                        for r in range(2):
                                            R = 2 * sl + r
                                            S.wait(v, "t_tr2", R + 1)
                                            inst = v.tensor_copy(
                                                xaT_own[:, 8 * r:8 * r + 8, sl, :],
                                                trp2[R % 2][:, :, :])
                                            S.inc(inst, "v_xt2")

                        @blk.scalar
                        def _(s):
                            for it in range(NS // 2 + 2):
                                for j2 in range(2):
                                    j = 2 * (it - 1) + j2
                                    if not 0 <= j < NS:
                                        continue
                                    pj = j % 2
                                    S.wait(s, "t_kv2", j + 1)
                                    if j >= 1:
                                        S.wait(s, "v_kr2", j)
                                    inst = s.copy(ksb2[:, pj, :], kvp2[pj][:, 0:KV])
                                    S.wait(s, "s_kc2", S.inc(inst, "s_kc2"))
                                    for g_ in range(G):
                                        inst = s.activation(junk2[:, HD * g_:HD * (g_ + 1)],
                                                            ksb2[:, pj, HD * g_:HD * (g_ + 1)],
                                                            AF.Square,
                                                            accum_out=ssb2[:, pj, g_:g_ + 1])
                                    S.selfsync(s, inst, "s_ss")
                                    inst = s.activation(sqb2[:, pj, :], ssb2[:, pj, :], AF.Ln,
                                                        bias=onesc[:, 2:3], scale=1.0)
                                    S.selfsync(s, inst, "s_ss")
                                    inst = s.activation(reck2[:, pj, :], sqb2[:, pj, :], AF.Exp,
                                                        bias=0.0, scale=-0.5)
                                    S.inc(inst, "s_rec2")
                                    inst = s.copy(v_own[:, j, :, :], kvp2[pj][:, KV:2 * KV])
                                    S.inc(inst, "s_vc2")
                                if it < NS // 2:
                                    for j2 in range(2):
                                        sl = 2 * it + j2
                                        S.wait(s, "v_mv2", sl + 1)
                                        inst = s.activation(lnb2[:, j2:j2 + 1], mv2[:, j2, 1:2],
                                                            AF.Ln, bias=onesc[:, 1:2], scale=1.0)
                                        S.selfsync(s, inst, "s_ss")
                                        inst = s.activation(rstd2[:, j2:j2 + 1],
                                                            lnb2[:, j2:j2 + 1],
                                                            AF.Exp, bias=0.0, scale=-0.5)
                                        S.inc(inst, "s_rs2")
                                for j2 in range(2):
                                    k2 = 2 * (it - 2) + j2
                                    if not 0 <= k2 < NS:
                                        continue
                                    S.wait(s, "t_kt2", k2 + 1)
                                    inst = s.copy(kT_own[:, :, k2, :], ktrp2[k2 % 2][:, :, :])
                                    S.inc(inst, "s_ke2")

                        @blk.tensor
                        def _(t):
                            S.wait(t, "s_ev", p1_fin["s_ev"])
                            S.wait(t, "s_ke", p1_fin["s_ke"])
                            S.wait(t, "v_xt", p1_fin["v_xt"])
                            for it in range(NS // 2 + 2):
                                for j2 in range(2):
                                    j = 2 * (it - 1) + j2
                                    if not 0 <= j < NS:
                                        continue
                                    pj = j % 2
                                    S.wait(t, "v_xt2", 2 * (j + 1))
                                    if j >= 2:
                                        S.wait(t, "s_kc2", j - 1)
                                        S.wait(t, "s_vc2", j - 1)
                                    for ct in range(16):
                                        t.matmul(kvp2[pj][:, 0:512], xaT_own[:, ct, j, :],
                                                 wkv_sb[:, ct, 0:512], start=(ct == 0), stop=False)
                                        t.matmul(kvp2[pj][:, 512:1024], xaT_own[:, ct, j, :],
                                                 wkv_sb[:, ct, 512:1024], start=(ct == 0), stop=False)
                                    t.matmul(kvp2[pj][:, 0:512], onesr[0:1, 0:128], bkv_sb[:, 0:512],
                                             start=False, stop=True)
                                    inst = t.matmul(kvp2[pj][:, 512:1024], onesr[0:1, 0:128],
                                                    bkv_sb[:, 512:1024], start=False, stop=True)
                                    S.inc(inst, "t_kv2")
                                if it < NS // 2:
                                    for j2 in range(2):
                                        sl = 2 * it + j2
                                        p = sl % 2
                                        S.wait(t, "v_xa2", sl + 1)
                                        for r in range(2):
                                            R = 2 * sl + r
                                            if R >= 2:
                                                S.wait(t, "v_xt2", R - 1)
                                            for c8 in range(8):
                                                inst = t.transpose(
                                                    trp2[R % 2][:, c8, :],
                                                    xa2[:, p,
                                                        128 * (8 * r + c8):128 * (8 * r + c8 + 1)],
                                                    ident[:, :])
                                            S.inc(inst, "t_tr2")
                                for j2 in range(2):
                                    k2 = 2 * (it - 1) + j2
                                    if not 0 <= k2 < NS:
                                        continue
                                    pk = k2 % 2
                                    S.wait(t, "v_kr2", k2 + 1)
                                    if k2 >= 2:
                                        S.wait(t, "s_ke2", k2 - 1)
                                    for g_ in range(G):
                                        inst = t.transpose(ktrp2[pk][:, g_, :],
                                                           kr2[:, pk, HD * g_:HD * (g_ + 1)],
                                                           ident[:, :])
                                    S.inc(inst, "t_kt2")

                with contextlib.ExitStack() as pB:
                    psbB = lambda n_, sh, dt: pB.enter_context(nc.psum_tensor(n_, list(sh), dt))
                    qpp = (psbB("qp0", [128, 512], F32), psbB("qp1", [128, 512], F32))
                    qtrp = (psbB("qtr0", [128, 512], BF16), psbB("qtr1", [128, 512], BF16))
                    junkp = pB.enter_context(nc.sbuf_tensor("junkpb", [128, 512], BF16))

                    with nc.Block("p1bB", no_gpsimd_drain=True) as blk:
                        @blk.gpsimd
                        def _(g):
                            wqr = wq.rearrange("(ct p) n -> p ct n", p=128)
                            for q4 in range(4):
                                if q4 >= 2:
                                    S.wait(g, "t_q", NS * (q4 - 1))
                                S.inc(g.dma_start(out=wqb4[:, q4 % 2, :, :],
                                                  in_=wqr[:, :, 512 * q4:512 * (q4 + 1)]),
                                      "d_wq%d" % (q4 % 2), 16)
                            S.wait(g, "d_cp", S.n["d_cp"])

                        @blk.tensor
                        def _(t):
                            S.wait(t, "s_kc2", NS)
                            S.wait(t, "s_vc2", NS)
                            S.wait(t, "s_ke2", NS)
                            S.wait(t, "v_xt2", 2 * NS)
                            NU = 4 * NS
                            for it in range(NU + 2):
                                u = it
                                if u < NU:
                                    q4, sl = u // NS, u % NS
                                    S.wait(t, "d_wq%d" % (q4 % 2), 16 * (q4 // 2 + 1))
                                    if u >= 2:
                                        S.wait(t, "s_qc", u - 1)
                                    for ct in range(16):
                                        t.matmul(qpp[u % 2][:, :], xaT_own[:, ct, sl, :],
                                                 wqb4[:, q4 % 2, ct, :],
                                                 start=(ct == 0), stop=False)
                                    inst = t.matmul(qpp[u % 2][:, :], onesr[0:1, 0:128],
                                                    bq_sb[:, 512 * q4:512 * (q4 + 1)],
                                                    start=False, stop=True)
                                    S.inc(inst, "t_q")
                                b2 = it - 2
                                if b2 >= 0:
                                    S.wait(t, "v_qr", b2 + 1)
                                    if b2 >= 2:
                                        S.wait(t, "v_qe", b2 - 1)
                                    for dq in range(4):
                                        inst = t.transpose(
                                            qtrp[b2 % 2][:, 128 * dq:128 * (dq + 1)],
                                            qr4[:, b2 % 2, 128 * dq:128 * (dq + 1)],
                                            ident[:, :])
                                    S.inc(inst, "t_qt")

                        @blk.scalar
                        def _(s):
                            NU = 4 * NS
                            for it in range(NU + 1):
                                b2 = it - 1
                                if b2 >= 0:
                                    pb_ = b2 % 2
                                    S.wait(s, "t_q", b2 + 1)
                                    if b2 >= 2:
                                        S.wait(s, "v_qh", b2 - 1)
                                    inst = s.copy(qsb4[:, pb_, :], qpp[pb_][:, :])
                                    S.wait(s, "s_qc", S.inc(inst, "s_qc"))
                                    for dq in range(4):
                                        inst = s.activation(
                                            junkp[:, 128 * dq:128 * (dq + 1)],
                                            qsb4[:, pb_, 128 * dq:128 * (dq + 1)],
                                            AF.Square, accum_out=qss4[:, pb_, dq:dq + 1])
                                    S.selfsync(s, inst, "s_ss")
                                    inst = s.activation(qln4[:, pb_, :], qss4[:, pb_, :],
                                                        AF.Ln, bias=onesc[:, 2:3], scale=1.0)
                                    S.selfsync(s, inst, "s_ss")
                                    inst = s.activation(qrec4[:, pb_, :], qln4[:, pb_, :],
                                                        AF.Exp, bias=0.0, scale=-0.5)
                                    S.inc(inst, "s_qs")

                        @blk.vector
                        def _(v):
                            NU = 4 * NS
                            for it in range(NU + 2):
                                b2 = it - 1
                                if 0 <= b2 < NU:
                                    pb_ = b2 % 2
                                    q4, sl = b2 // NS, b2 % NS
                                    S.wait(v, "s_qs", b2 + 1)
                                    inst = v.tensor_mul(qrm[:, :], qrec4[:, pb_, :],
                                                        alpha_bc[:, 4 * q4:4 * q4 + 4])
                                    S.selfsync(v, inst, "v_ss")
                                    for dq in range(4):
                                        inst = v.tensor_scalar(
                                            qsb4[:, pb_, 128 * dq:128 * (dq + 1)],
                                            qsb4[:, pb_, 128 * dq:128 * (dq + 1)],
                                            qrm[:, dq:dq + 1], None, ALU.mult)
                                    S.selfsync(v, inst, "v_ss")
                                    ct_ = coso[:, sl, :]
                                    st_ = sino[:, sl, :]
                                    for dq in range(4):
                                        x1 = qsb4[:, pb_, 128 * dq:128 * dq + 64]
                                        x2_ = qsb4[:, pb_, 128 * dq + 64:128 * dq + 128]
                                        v.tensor_mul(vjunk2[:, dq, 0:64], x1, ct_)
                                        v.tensor_mul(vjunk2[:, dq, 64:128], x2_, st_)
                                        v.tensor_mul(vjunk2[:, dq, 128:192], x1, st_)
                                        inst = v.tensor_mul(vjunk2[:, dq, 192:256], x2_, ct_)
                                    S.wait(v, "v_qh", S.inc(inst, "v_qh"))
                                    if b2 >= 2:
                                        S.wait(v, "t_qt", b2 - 1)
                                    for dq in range(4):
                                        v.tensor_sub(qr4[:, pb_, 128 * dq:128 * dq + 64],
                                                     vjunk2[:, dq, 0:64],
                                                     vjunk2[:, dq, 64:128])
                                        inst = v.tensor_add(
                                            qr4[:, pb_, 128 * dq + 64:128 * (dq + 1)],
                                            vjunk2[:, dq, 128:192], vjunk2[:, dq, 192:256])
                                    S.inc(inst, "v_qr")
                                c = it - 2
                                if c >= 0:
                                    q4c, slc = c // NS, c % NS
                                    S.wait(v, "t_qt", c + 1)
                                    inst = v.tensor_copy(
                                        qT[:, 4 * q4c:4 * q4c + 4, slc, :],
                                        qtrp[c % 2].rearrange("p (h d) -> p h d", d=128))
                                    S.inc(inst, "v_qe")

        eskv.close()
        p1b_fin = S.snap("v_qe", "s_ke2", "s_vc2", "t_q", "d_wq0", "d_wq1", "s_qs", "v_ss")

        x2 = sb("x2b", [128, NS, C], BF16)
        eswo = contextlib.ExitStack()
        woA = eswo.enter_context(nc.sbuf_tensor("woA", [128, 2, 16, 512], BF16))

        if stage >= 3:
            if shared_kv:
                nc.all_core_barrier()
            def CNT(sl, js, g_):
                return 4 * (JSOFF[sl] + js) + g_ + 1

            strips = [(sl_, n_) for sl_ in range(NS) for n_ in range(sl_ + 1)]
            strip_idx = {sn: m for m, sn in enumerate(strips)}

            def strip_consumed_cnt(m):
                sl_, n_ = strips[m]
                last_js = min(8 * n_ + 8, NJS[sl_] - 1)
                return 4 * (JSOFF[sl_] + last_js) + 4

            with contextlib.ExitStack() as p2x:
                lsb = lambda n_, sh, dt: p2x.enter_context(nc.sbuf_tensor(n_, list(sh), dt))
                psb = lambda n_, sh, dt: p2x.enter_context(nc.psum_tensor(n_, list(sh), dt))
                kst = lsb("kst", [128, G, 2, 8, 128], BF16)
                vst = lsb("vst", [128, 2, 8, G, 128], BF16)
                expS = lsb("expS", [128, G, 2, 512], BF16)
                dacc = lsb("dacc", [128, G, 512], F32)
                recs = lsb("recs", [1, G, 512], F32)
                bcsb = lsb("bcsb", [128, G, 512], F32)
                scp = [psb("scp%d" % g_, [128, 512], F32) for g_ in range(G)]
                yp = [psb("yp%d" % g_, [128, 512], F32) for g_ in range(G)]

                with nc.Block("p2", no_gpsimd_drain=True) as blk:
                    @blk.gpsimd
                    def _(g):
                        wor = wo.rearrange("p (cc dt c) -> p cc dt c", dt=16, c=512)
                        for cc4 in range(2):
                            S.inc(g.dma_start(out=woA[:, cc4, :, :],
                                              in_=wor[:, cc4, :, :]), "d_wo%d" % cc4, 16)

                    @blk.sync
                    def _(sy):
                        S.wait(sy, "t_q", p1b_fin["t_q"])
                        kvr = kv_k.rearrange("g p s -> p g s")
                        vvr = kv_v.rearrange("(n p) d -> p n d", p=128)
                        for m, (sl_, n_) in enumerate(strips):
                            par = m % 2
                            if m >= 2:
                                S.wait(sy, "t_av", strip_consumed_cnt(m - 2))
                            S.inc(sy.dma_start(out=kst[:, :, par, :, :],
                                               in_=kvr[:, :, 1024 * n_:1024 * (n_ + 1)]),
                                  "d_ks%d" % par, 16)
                            S.inc(sy.dma_start(out=vst[:, par, :, :, :],
                                               in_=vvr[:, 8 * n_:8 * (n_ + 1), :]),
                                  "d_vs%d" % par, 16)

                    @blk.tensor
                    def _(t):
                        S.wait(t, "v_qe", p1b_fin["v_qe"])
                        S.wait(t, "s_ke2", p1b_fin["s_ke2"])
                        S.wait(t, "s_vc2", p1b_fin["s_vc2"])
                        for sl in range(NS):
                            for js in range(NJS[sl] + 1):
                                if js < NJS[sl]:
                                    for g_ in range(G):
                                        if js == 0:
                                            if sl >= 1:
                                                S.wait(t, "v_ye", 4 * (sl - 1) + g_ + 1)
                                            lhs = kT_own[:, g_, sl, :]
                                        else:
                                            S.wait(t, "s_ex", CNT(sl, js - 1, g_))
                                            m = strip_idx[(sl, (js - 1) // 8)]
                                            S.wait(t, "d_ks%d" % (m % 2), 16 * (m // 2 + 1))
                                            lhs = kst[:, g_, m % 2, (js - 1) % 8, :]
                                        inst = t.matmul(scp[g_][:, :], lhs,
                                                        qT[:, 4 * g_:4 * g_ + 4, sl, :],
                                                        start=True, stop=True)
                                        S.inc(inst, "t_sc")
                                if js >= 1:
                                    ja = js - 1
                                    for g_ in range(G):
                                        S.wait(t, "s_ex", CNT(sl, ja, g_))
                                        if ja == 0:
                                            lhs = v_own[:, sl, g_, :]
                                        else:
                                            m = strip_idx[(sl, (ja - 1) // 8)]
                                            S.wait(t, "d_vs%d" % (m % 2), 16 * (m // 2 + 1))
                                            lhs = vst[:, m % 2, (ja - 1) % 8, g_, :]
                                        inst = t.matmul(yp[g_][:, :], lhs,
                                                        expS[:, g_, ja % 2, :],
                                                        start=(ja == 0),
                                                        stop=(ja == NJS[sl] - 1))
                                        S.inc(inst, "t_av")
                            for g_ in range(G):
                                S.wait(t, "v_dn", CNT(sl, NJS[sl] - 1, g_))
                                S.wait(t, "s_ex", CNT(sl, NJS[sl] - 1, g_))
                                inst = t.matmul(scp[g_][0:1, :], onesc[:, 0:1], dacc[:, g_, :],
                                                start=True, stop=True)
                                S.inc(inst, "t_dr")
                            for g_ in range(G):
                                S.wait(t, "v_rc", 4 * sl + g_ + 1)
                                inst = t.matmul(scp[g_][:, :], oneslh[:, :], recs[0:1, g_, :],
                                                start=True, stop=True)
                                S.inc(inst, "t_bc")

                    @blk.scalar
                    def _(s):
                        for sl in range(NS):
                            for js in range(NJS[sl]):
                                for g_ in range(G):
                                    S.wait(s, "t_sc", CNT(sl, js, g_))
                                    if js == 0:
                                        S.wait(s, "v_tri", 4 * sl + g_ + 1)
                                    if js >= 2:
                                        S.wait(s, "t_av", CNT(sl, js - 2, g_))
                                        S.wait(s, "v_dn", CNT(sl, js - 2, g_))
                                    col = JSOFF[sl] + js
                                    inst = s.activation(expS[:, g_, js % 2, :], scp[g_][:, :],
                                                        AF.Exp,
                                                        bias=bias_tab[:, g_, col:col + 1],
                                                        scale=1.0)
                                    S.inc(inst, "s_ex")

                    @blk.vector
                    def _(v):
                        for sl in range(NS):
                            for js in range(NJS[sl]):
                                if js == 0:
                                    for g_ in range(G):
                                        S.wait(v, "t_sc", CNT(sl, 0, g_))
                                        inst = v.tensor_add(scp[g_][:, :], scp[g_][:, :],
                                                            trim[:, :])
                                        S.inc(inst, "v_tri")
                                for g_ in range(G):
                                    S.wait(v, "s_ex", CNT(sl, js, g_))
                                    if js == 0:
                                        inst = v.tensor_copy(dacc[:, g_, :], expS[:, g_, 0, :])
                                    else:
                                        S.wait(v, "v_dn", CNT(sl, js - 1, g_))
                                        inst = v.tensor_add(dacc[:, g_, :], dacc[:, g_, :],
                                                            expS[:, g_, js % 2, :])
                                    S.inc(inst, "v_dn")
                            for g_ in range(G):
                                S.wait(v, "t_dr", 4 * sl + g_ + 1)
                                inst = v.tensor_copy(recs[0:1, g_, :], scp[g_][0:1, :])
                                S.selfsync(v, inst, "v_ss")
                                inst = v.reciprocal(recs[0:1, g_, :], recs[0:1, g_, :])
                                S.selfsync(v, inst, "v_rc")
                            for g_ in range(G):
                                S.wait(v, "t_bc", 4 * sl + g_ + 1)
                                inst = v.tensor_copy(bcsb[:, g_, :], scp[g_][:, :])
                                S.selfsync(v, inst, "v_ss")
                                S.wait(v, "t_av", CNT(sl, NJS[sl] - 1, g_))
                                inst = v.tensor_mul(qT[:, 4 * g_:4 * g_ + 4, sl, :],
                                                    yp[g_][:, :].rearrange("p (h d) -> p h d", h=4),
                                                    bcsb[:, g_, :].rearrange("p (h d) -> p h d", h=4))
                                S.inc(inst, "v_ye")

        p2_fin = S.snap("v_ye", "t_bc", "s_ex", "v_dn", "v_rc", "d_ks0", "d_ks1",
                        "d_vs0", "d_vs1", "t_av", "t_sc")

        if stage >= 4:
            with contextlib.ExitStack() as p3x:
                lsb = lambda n_, sh, dt: p3x.enter_context(nc.sbuf_tensor(n_, list(sh), dt))
                psb = lambda n_, sh, dt: p3x.enter_context(nc.psum_tensor(n_, list(sh), dt))
                woB = lsb("woB", [128, 2, 16, 512], BF16)
                xr = lsb("xr", [128, 2, C], F32)
                usb = lsb("usb", [128, C], F32)
                stats3 = lsb("stats3", [128, 24], F32)
                mv3 = lsb("mv3", [128, 2], F32)
                lnb3 = lsb("lnb3", [128, 1], F32)
                rstd3 = lsb("rstd3", [128, 1], F32)
                op = (psb("op0", [128, C], F32), psb("op1", [128, C], F32))

                with nc.Block("p3", no_gpsimd_drain=True) as blk:
                    @blk.gpsimd
                    def _(g):
                        for nm in ("d_ks0", "d_ks1", "d_vs0", "d_vs1"):
                            S.wait(g, nm, p2_fin[nm])
                        wor = wo.rearrange("p (cc dt c) -> p cc dt c", dt=16, c=512)
                        for cc4 in range(2, 4):
                            S.inc(g.dma_start(out=woB[:, cc4 - 2, :, :],
                                              in_=wor[:, cc4, :, :]), "d_wo%d" % cc4, 16)

                    @blk.sync
                    def _(sy):
                        xo_t = x_own.rearrange("(n p) c -> p n c", p=128)
                        for sl in range(NS):
                            if sl >= 2:
                                S.wait(sy, "v_o3", sl - 1)
                            S.inc(sy.dma_start(out=xr[:, sl % 2, :], in_=xo_t[:, sl, :]),
                                  "d_x3%d" % (sl % 2), 16)

                    @blk.tensor
                    def _(t):
                        S.wait(t, "v_ye", p2_fin["v_ye"])
                        for sl in range(NS):
                            if sl >= 2:
                                S.wait(t, "v_o3", sl - 1)
                            for cc in range(4):
                                if sl == 0:
                                    S.wait(t, "d_wo%d" % cc, 16)
                                wocc = woA[:, cc, :, :] if cc < 2 else woB[:, cc - 2, :, :]
                                for h in range(H):
                                    t.matmul(op[sl % 2][:, 512 * cc:512 * (cc + 1)],
                                             qT[:, h, sl, :],
                                             wocc[:, h, :],
                                             start=(h == 0), stop=False)
                                inst = t.matmul(op[sl % 2][:, 512 * cc:512 * (cc + 1)],
                                                onesr[0:1, 0:128],
                                                bo_sb[:, 512 * cc:512 * (cc + 1)],
                                                start=False, stop=True)
                            S.inc(inst, "t_o3")

                    @blk.scalar
                    def _(s):
                        for sl in range(NS):
                            S.wait(s, "v_mv3", sl + 1)
                            inst = s.activation(lnb3[:, :], mv3[:, 1:2], AF.Ln,
                                                bias=onesc[:, 1:2], scale=1.0)
                            S.selfsync(s, inst, "s_ss")
                            inst = s.activation(rstd3[:, :], lnb3[:, :], AF.Exp,
                                                bias=0.0, scale=-0.5)
                            S.inc(inst, "s_rs3")

                    @blk.vector
                    def _(v):
                        for sl in range(NS):
                            p = sl % 2
                            S.wait(v, "t_o3", sl + 1)
                            for cch in range(4):
                                inst = v.bn_stats(stats3[:, 6 * cch:6 * cch + 6],
                                                  op[p][:, 512 * cch:512 * cch + 512])
                            S.selfsync(v, inst, "v_ss")
                            inst = v.bn_aggr(mv3[:, :], stats3[:, :])
                            S.inc(inst, "v_mv3")
                            S.wait(v, "s_rs3", sl + 1)
                            inst = v.tensor_scalar(usb[:, :], op[p][:, :], mv3[:, 0:1],
                                                   rstd3[:, 0:1], ALU.subtract, ALU.mult)
                            S.selfsync(v, inst, "v_ss")
                            S.wait(v, "d_x3%d" % p, 16 * (sl // 2 + 1))
                            inst = v.tensor_add(x2[:, sl, :], usb[:, :], xr[:, p, :])
                            S.inc(inst, "v_o3")

        p3_fin = S.snap("v_o3", "t_o3", "d_wo0", "d_wo1", "d_wo2", "d_wo3",
                        "d_x30", "d_x31")
        if stage >= 4:
            eswo.close()
            esq.close()
            esa.close()

        if stage >= 5:
            with contextlib.ExitStack() as p4x:
                lsb = lambda n_, sh, dt: p4x.enter_context(nc.sbuf_tensor(n_, list(sh), dt))
                xfT = lsb("xfT", [128, 16, NS * 128], BF16)
                acc = lsb("acc", [128, NS, C], BF16)
                C_sb = lsb("C_sb", [128, NS, 8], F32)
                gate = lsb("gate", [128, NS], F32)
                probs = lsb("probs", [128, 8], F32)
                esb = lsb("esb", [128, 8], F32)
                tju = lsb("tju", [128, 8], F32)
                negm = lsb("negm", [128, 1], F32)
                m1b = lsb("m1b", [128, 1], F32)
                m2b = lsb("m2b", [128, 1], F32)
                ssum = lsb("ssum", [128, 1], F32)
                mskb = lsb("mskb", [128, 8], F32)
                stats4 = lsb("stats4", [128, 24], F32)
                mv4 = lsb("mv4", [128, 2], F32)
                lnb4 = lsb("lnb4", [128, 1], F32)
                rstd4 = lsb("rstd4", [128, 1], F32)
                wr9_sb = lsb("wr9_sb", [128, 16, 9], BF16)
                TOK = NS * 128
                TCH = [(lo, min(lo + 512, TOK)) for lo in range(0, TOK, 512)]
                p4e = contextlib.ExitStack()
                lse = lambda n_, sh, dt: p4e.enter_context(nc.sbuf_tensor(n_, list(sh), dt))
                act = lse("act", [128, 2, NF, TOK], BF16)
                gsb = lse("gsb", [128, TOK], BF16)
                wgub = lse("wgub", [128, 2, 16, 256], BF16)
                wdb = lse("wdb", [128, 2, NF, 1024], BF16)

                with contextlib.ExitStack() as pp:
                    trF = (pp.enter_context(nc.psum_tensor("trF0", [128, 4, 128], BF16)),
                           pp.enter_context(nc.psum_tensor("trF1", [128, 4, 128], BF16)))
                    xfb = pp.enter_context(nc.sbuf_tensor("xfb", [128, 1, C], BF16))
                    r9 = pp.enter_context(nc.psum_tensor("r9", [128, 16], F32))
                    with nc.Block("p4a", no_gpsimd_drain=True) as blk:
                        @blk.gpsimd
                        def _(g):
                            for cc4 in range(4):
                                S.wait(g, "d_wo%d" % cc4, p3_fin["d_wo%d" % cc4])
                            S.wait(g, "d_x30", p3_fin["d_x30"])
                            S.wait(g, "d_x31", p3_fin["d_x31"])
                            S.inc(g.dma_start(out=wr9_sb[...],
                                              in_=wr9.rearrange("(ct p) n -> p ct n", p=128)),
                                  "d_w4", 16)
                            for hh in range(2):
                                S.inc(g.dma_start(
                                    out=wdb[:, hh, :, :],
                                    in_=wdd_d[0, hh].rearrange(
                                        "p (ft c) -> p ft c", c=1024)),
                                      "d_dw%d" % hh, 16)
                            for ft in range(2):
                                S.inc(g.dma_start(
                                    out=wgub[:, ft % 2, :, :],
                                    in_=wgu_d[0, ft].rearrange(
                                        "p (ct f) -> p ct f", f=256)),
                                      "d_gw%d" % (ft % 2), 32)

                        @blk.vector
                        def _(v):
                            for sl in range(NS):
                                p = sl % 2
                                for cch in range(4):
                                    inst = v.bn_stats(stats4[:, 6 * cch:6 * cch + 6],
                                                      x2[:, sl, 512 * cch:512 * cch + 512])
                                S.selfsync(v, inst, "v_ss")
                                inst = v.bn_aggr(mv4[:, :], stats4[:, :])
                                S.inc(inst, "v_mv4")
                                S.wait(v, "s_rs4", sl + 1)
                                if sl >= 1:
                                    S.wait(v, "t_tr4", 4 * sl)
                                inst = v.tensor_scalar(xfb[:, 0, :], x2[:, sl, :], mv4[:, 0:1],
                                                       rstd4[:, 0:1], ALU.subtract, ALU.mult)
                                S.inc(inst, "v_xf4")
                                for r in range(4):
                                    S.wait(v, "t_tr4", 4 * sl + r + 1)
                                    inst = v.tensor_copy(xfT[:, 4 * r:4 * r + 4,
                                                             128 * sl:128 * (sl + 1)],
                                                         trF[(4 * sl + r) % 2][:, :, :])
                                    S.inc(inst, "v_xt4")
                                S.wait(v, "t_r4", sl + 1)
                                inst = v.reduce_max(negm[:, :], r9[:, 0:8], axis=AX.X,
                                                    negate=True)
                                S.selfsync(v, inst, "v_ng4")
                                S.wait(v, "s_e4", sl + 1)
                                inst = v.reduce_sum(ssum[:, :], esb[:, :], axis=AX.X)
                                S.selfsync(v, inst, "v_ss")
                                inst = v.reciprocal(ssum[:, :], ssum[:, :])
                                S.selfsync(v, inst, "v_ss")
                                inst = v.tensor_scalar(probs[:, :], esb[:, :], ssum[:, 0:1],
                                                       None, ALU.mult)
                                S.selfsync(v, inst, "v_ss")
                                inst = v.reduce_max(m1b[:, :], probs[:, :], axis=AX.X)
                                S.selfsync(v, inst, "v_ss")
                                inst = v.tensor_scalar(mskb[:, :], probs[:, :], m1b[:, 0:1],
                                                       None, ALU.is_equal)
                                S.selfsync(v, inst, "v_ss")
                                inst = v.scalar_tensor_tensor(tju[:, :], mskb[:, :], -1e30,
                                                              probs[:, :], ALU.mult, ALU.add)
                                S.selfsync(v, inst, "v_ss")
                                inst = v.reduce_max(m2b[:, :], tju[:, :], axis=AX.X)
                                S.selfsync(v, inst, "v_ss")
                                inst = v.tensor_scalar(mskb[:, :], probs[:, :], m2b[:, 0:1],
                                                       None, ALU.is_ge)
                                S.selfsync(v, inst, "v_ss")
                                inst = v.tensor_mul(C_sb[:, sl, :], mskb[:, :], probs[:, :])
                                S.inc(inst, "v_c4")

                        @blk.scalar
                        def _(s):
                            for sl in range(NS):
                                S.wait(s, "v_mv4", sl + 1)
                                inst = s.activation(lnb4[:, :], mv4[:, 1:2], AF.Ln,
                                                    bias=onesc[:, 1:2], scale=1.0)
                                S.selfsync(s, inst, "s_ss")
                                inst = s.activation(rstd4[:, :], lnb4[:, :], AF.Exp,
                                                    bias=0.0, scale=-0.5)
                                S.inc(inst, "s_rs4")
                                S.wait(s, "v_ng4", sl + 1)
                                inst = s.activation(esb[:, :], r9[:, 0:8], AF.Exp,
                                                    bias=negm[:, 0:1], scale=1.0)
                                S.wait(s, "s_e4g", S.inc(inst, "s_e4g"))
                                inst = s.activation(gate[:, sl:sl + 1], r9[:, 8:9], AF.Sigmoid)
                                S.inc(inst, "s_e4")

                        @blk.tensor
                        def _(t):
                            S.wait(t, "v_o3", p3_fin["v_o3"])
                            S.wait(t, "d_w4", 16)
                            for sl in range(NS):
                                S.wait(t, "v_xf4", sl + 1)
                                for r in range(4):
                                    R = 4 * sl + r
                                    if R >= 2:
                                        S.wait(t, "v_xt4", R - 1)
                                    for c4 in range(4):
                                        inst = t.transpose(
                                            trF[R % 2][:, c4, :],
                                            xfb[:, 0,
                                                128 * (4 * r + c4):128 * (4 * r + c4 + 1)],
                                            ident[:, :])
                                    S.inc(inst, "t_tr4")
                                S.wait(t, "v_xt4", 4 * (sl + 1))
                                if sl >= 1:
                                    S.wait(t, "s_e4", sl)
                                for ct in range(16):
                                    inst = t.matmul(r9[:, 0:9],
                                                    xfT[:, ct, 128 * sl:128 * (sl + 1)],
                                                    wr9_sb[:, ct, :],
                                                    start=(ct == 0), stop=(ct == 15))
                                S.inc(inst, "t_r4")

                NEXP = E + 4
                with contextlib.ExitStack() as pe:
                    pg = pe.enter_context(nc.psum_tensor("pg", [128, TOK], F32))
                    pu = pe.enter_context(nc.psum_tensor("pu", [128, TOK], F32))
                    pd = (pe.enter_context(nc.psum_tensor("pd0", [128, 1024], F32)),
                          pe.enter_context(nc.psum_tensor("pd1", [128, 1024], F32)))

                    NCH = 2 * NS
                    with nc.Block("p4b", no_gpsimd_drain=True) as blk:
                        @blk.gpsimd
                        def _(g):
                            def wdload(e, hh):
                                if e >= 1:
                                    S.wait(g, "t_d", 2 * NS * (e - 1) + NS * (hh + 1))
                                S.inc(g.dma_start(
                                    out=wdb[:, hh, :, :],
                                    in_=wdd_d[e, hh].rearrange(
                                        "p (ft c) -> p ft c", c=1024)),
                                      "d_dw%d" % hh, 16)

                            for e in range(NEXP):
                                for ft in range(NF):
                                    gidx = e * NF + ft
                                    if gidx < 2:
                                        continue
                                    if gidx >= 2:
                                        S.wait(g, "t_gu", 2 * (gidx - 1))
                                    S.inc(g.dma_start(
                                        out=wgub[:, gidx % 2, :, :],
                                        in_=wgu_d[e, ft].rearrange(
                                            "p (ct f) -> p ct f", f=256)),
                                          "d_gw%d" % (gidx % 2), 32)
                                    if e >= 2 and ft == 1:
                                        wdload(e - 1, 1)
                                    if e >= 1 and ft == 7:
                                        wdload(e, 0)
                            wdload(NEXP - 1, 1)

                        @blk.tensor
                        def _(t):
                            def dchunk(ed, k):
                                cc, tt = k // NS, k % NS
                                didx = 2 * NS * ed + NS * cc + tt
                                par = didx % 2
                                if k == 0:
                                    S.wait(t, "v_ac4", NF * (ed + 1))
                                if tt == 0:
                                    S.wait(t, "d_dw%d" % cc, 16 * (ed + 1))
                                if didx >= 2:
                                    S.wait(t, "v_da4", didx - 1)
                                for ft in range(NF):
                                    t.matmul(pd[par][:, 0:512],
                                             act[:, ed % 2, ft, 128 * tt:128 * (tt + 1)],
                                             wdb[:, cc, ft, 0:512],
                                             start=(ft == 0), stop=(ft == NF - 1))
                                    inst = t.matmul(pd[par][:, 512:1024],
                                                    act[:, ed % 2, ft,
                                                        128 * tt:128 * (tt + 1)],
                                                    wdb[:, cc, ft, 512:1024],
                                                    start=(ft == 0), stop=(ft == NF - 1))
                                S.inc(inst, "t_d")

                            S.wait(t, "v_xt4", 4 * NS)
                            for e in range(NEXP):
                                dk = 0
                                for ft in range(NF):
                                    gidx = e * NF + ft
                                    S.wait(t, "d_gw%d" % (gidx % 2), 32 * (gidx // 2 + 1))
                                    S.wait(t, "s_si4", gidx)
                                    for ct in range(16):
                                        for lo, hi in TCH:
                                            inst = t.matmul(pg[:, lo:hi],
                                                            wgub[:, gidx % 2, ct, 0:128],
                                                            xfT[:, ct, lo:hi], start=(ct == 0),
                                                            stop=(ct == 15))
                                    S.inc(inst, "t_gu")
                                    if e >= 1 and dk < NCH:
                                        dchunk(e - 1, dk)
                                        dk += 1
                                    S.wait(t, "v_ac4", gidx)
                                    for ct in range(16):
                                        for lo, hi in TCH:
                                            inst = t.matmul(pu[:, lo:hi],
                                                            wgub[:, gidx % 2, ct, 128:256],
                                                            xfT[:, ct, lo:hi], start=(ct == 0),
                                                            stop=(ct == 15))
                                    S.inc(inst, "t_gu")
                                    if e >= 1 and ft < NCH - NF:
                                        dchunk(e - 1, dk)
                                        dk += 1
                                while e >= 1 and dk < NCH:
                                    dchunk(e - 1, dk)
                                    dk += 1
                            for k in range(NCH):
                                dchunk(NEXP - 1, k)

                        @blk.scalar
                        def _(s):
                            for e in range(NEXP):
                                for ft in range(NF):
                                    gidx = e * NF + ft
                                    S.wait(s, "t_gu", 2 * gidx + 1)
                                    S.wait(s, "v_ac4", gidx)
                                    inst = s.activation(gsb[:, :], pg[:, :], AF.Silu)
                                    S.inc(inst, "s_si4")

                        @blk.vector
                        def _(v):
                            for tt in range(NS):
                                inst = v.memset(acc[:, tt, :], 0.0)
                            S.inc(inst, "v_ai4")

                            def accupd(ed, k):
                                cc, tt = k // NS, k % NS
                                didx = 2 * NS * ed + NS * cc + tt
                                par = didx % 2
                                S.wait(v, "t_d", didx + 1)
                                if ed == 0:
                                    S.wait(v, "v_ai4", 1)
                                else:
                                    S.wait(v, "v_da4",
                                           2 * NS * (ed - 1) + NS * cc + tt + 1)
                                scal = (C_sb[:, tt, ed:ed + 1] if ed < E
                                        else gate[:, tt:tt + 1])
                                inst = v.scalar_tensor_tensor(
                                    acc[:, tt, 1024 * cc:1024 * (cc + 1)],
                                    pd[par][:, :], scal,
                                    acc[:, tt, 1024 * cc:1024 * (cc + 1)],
                                    ALU.mult, ALU.add)
                                S.inc(inst, "v_da4")

                            for e in range(NEXP):
                                dk = 0
                                for ft in range(NF):
                                    gidx = e * NF + ft
                                    if e >= 2 and ft == 0:
                                        S.wait(v, "t_d", 2 * NS * (e - 1))
                                    S.wait(v, "s_si4", gidx + 1)
                                    S.wait(v, "t_gu", 2 * gidx + 2)
                                    inst = v.tensor_mul(act[:, e % 2, ft, :],
                                                        pu[:, :], gsb[:, :])
                                    S.inc(inst, "v_ac4")
                                    if e >= 1 and dk < NCH:
                                        accupd(e - 1, dk)
                                        dk += 1
                                    if e >= 1 and ft < NCH - NF:
                                        accupd(e - 1, dk)
                                        dk += 1
                                while e >= 1 and dk < NCH:
                                    accupd(e - 1, dk)
                                    dk += 1
                            for k in range(NCH):
                                accupd(NEXP - 1, k)

                p4e.close()
                osb = lsb("osb", [128, 1, C], F32)
                p45 = S.snap("v_mv4", "s_rs4", "v_o3", "d_x30")
                with nc.Block("p5", no_gpsimd_drain=True) as blk:
                    @blk.vector
                    def _(v):
                        for sl in range(NS):
                            p = sl % 2
                            for cch in range(4):
                                inst = v.bn_stats(stats4[:, 6 * cch:6 * cch + 6],
                                                  acc[:, sl, 512 * cch:512 * cch + 512])
                            S.selfsync(v, inst, "v_ss")
                            inst = v.bn_aggr(mv4[:, :], stats4[:, :])
                            S.inc(inst, "v_mv4")
                            S.wait(v, "s_rs4", p45["s_rs4"] + sl + 1)
                            if sl >= 1:
                                S.wait(v, "d_x30", p45["d_x30"] + 16 * sl)
                            inst = v.tensor_scalar(osb[:, 0, :], acc[:, sl, :], mv4[:, 0:1],
                                                   rstd4[:, 0:1], ALU.subtract, ALU.mult)
                            S.selfsync(v, inst, "v_ss")
                            inst = v.tensor_add(osb[:, 0, :], osb[:, 0, :], x2[:, sl, :])
                            S.inc(inst, "v_o3")

                    @blk.scalar
                    def _(s):
                        for sl in range(NS):
                            S.wait(s, "v_mv4", p45["v_mv4"] + sl + 1)
                            inst = s.activation(lnb4[:, :], mv4[:, 1:2], AF.Ln,
                                                bias=onesc[:, 1:2], scale=1.0)
                            S.selfsync(s, inst, "s_ss")
                            inst = s.activation(rstd4[:, :], lnb4[:, :], AF.Exp,
                                                bias=0.0, scale=-0.5)
                            S.inc(inst, "s_rs4")

                    @blk.sync
                    def _(sy):
                        oo = out_own.rearrange("(n p) c -> p n c", p=128)
                        for sl in range(NS):
                            S.wait(sy, "v_o3", p45["v_o3"] + sl + 1)
                            S.inc(sy.dma_start(out=oo[:, sl, :], in_=osb[:, 0, :]),
                                  "d_x30", 16)
                        S.wait(sy, "d_x30", S.n["d_x30"])

        if stage < 5:
            if "kv" in debug:
                dout("kv_k", [G, 128, T // 2], BF16)
                dout("kv_v", [T // 2, KV], BF16)
            if "qkv" in debug:
                dout("qT", [128, H, NS, 128], BF16)
                dout("kT_own", [128, G, NS, 128], BF16)
                dout("v_own", [128, NS, G, 128], BF16)
            with nc.Block("fin", no_gpsimd_drain=True) as blk:
                @blk.sync
                def _(sy):
                    waits = []
                    if stage >= 1:
                        waits += [("d_cp", p1_fin["d_cp"])]
                    if stage == 1:
                        waits += [("d_w2", 32)]
                    if stage >= 2 and stage < 3:
                        waits += [("v_qe", p1b_fin["v_qe"]), ("s_ke2", NS), ("s_vc2", NS)]
                    if stage >= 3:
                        waits += [("v_ye", p2_fin["v_ye"])]
                    if stage == 3:
                        waits += [("d_wo%d" % i_, 16) for i_ in range(4)]
                    for nm, val in waits:
                        S.wait(sy, nm, val)
                    if "kv" in debug:
                        S.inc(sy.dma_start(out=dbg["kv_k"][...], in_=kv_k_loc[...]), "d_db", 16)
                        S.inc(sy.dma_start(out=dbg["kv_v"][...], in_=kv_v_loc[...]), "d_db", 16)
                    if "qkv" in debug:
                        S.inc(sy.dma_start(out=dbg["qT"][...], in_=qT[...]), "d_db", 16)
                        S.inc(sy.dma_start(out=dbg["kT_own"][...], in_=kT_own[...]), "d_db", 16)
                        S.inc(sy.dma_start(out=dbg["v_own"][...], in_=v_own[...]), "d_db", 16)
                    if S.n.get("d_db", 0):
                        S.wait(sy, "d_db", S.n["d_db"])
        if stage < 4:
            eswo.close()
            esq.close()
            esa.close()
        if stage < 2:
            eskv.close()
    return nc, dbg


def _host_consts():
    perm = np.concatenate([np.arange(0, 128, 2), np.arange(1, 128, 2)])
    inv = 1.0 / (10000.0 ** (np.arange(0, HD, 2, dtype=np.float64) / HD))
    pos = np.arange(T, dtype=np.float64)
    fr = pos[:, None] * inv[None, :]
    cos_t = np.cos(fr).astype(np.float32)
    sin_t = np.sin(fr).astype(np.float32)
    ident = np.eye(128, dtype=ml_dtypes.bfloat16)
    ones_row = np.ones((1, 512), dtype=ml_dtypes.bfloat16)
    ones_col = np.concatenate([np.ones((128, 1), np.float32),
                               np.full((128, 1), EPS_LN, np.float32),
                               np.full((128, 1), 1e-12, np.float32)], axis=1)
    ones_lhs = np.ones((1, 128), dtype=np.float32)
    tri = np.zeros((128, 512), dtype=np.float32)
    t128 = np.where(np.arange(128)[:, None] <= np.arange(128)[None, :], 0.0, TRI)
    for h in range(4):
        tri[:, 128 * h:128 * (h + 1)] = t128
    return perm, cos_t, sin_t, ident, ones_row, ones_col, ones_lhs, tri.astype(ml_dtypes.bfloat16)


def _valid_table(pid):
    val = np.zeros((NVAL,), dtype=np.float32)
    idx = 0
    for j in range(NS):
        t = 8 * j + pid
        for js in range(NJS[j]):
            val[idx] = 1.0 if (js == 0 or (js - 1 < t)) else 0.0
            idx += 1
    return np.broadcast_to(val[None, :], (128, NVAL)).copy()


def _pack_wgu(wg, wu, sg, su):
    gs = np.concatenate([wg, sg.T.reshape(4, F, C).transpose(0, 2, 1)], axis=0)
    us = np.concatenate([wu, su.T.reshape(4, F, C).transpose(0, 2, 1)], axis=0)
    gsr = gs.reshape(12, 16, 128, NF, 128)
    usr = us.reshape(12, 16, 128, NF, 128)
    pk = np.stack([gsr, usr], axis=4)
    return np.ascontiguousarray(
        pk.transpose(0, 3, 2, 1, 4, 5).reshape(12, NF, 128, 16 * 256))


def _pack_wdd(wd, sd):
    ds = np.concatenate([wd, sd.reshape(4, F, C)], axis=0)
    dsr = ds.reshape(12, NF, 128, 2, 1024)
    return np.ascontiguousarray(
        dsr.transpose(0, 3, 2, 1, 4).reshape(12, 2, 128, NF * 1024))


def _pack_wo(wo):
    return np.ascontiguousarray(
        wo.reshape(16, 128, 4, 512).transpose(1, 2, 0, 3).reshape(128, -1))


def _prep_in_maps(inputs):
    gi = lambda k: np.ascontiguousarray(np.asarray(inputs[k], dtype=np.float32))
    x = gi("L0_attn_x").reshape(T, C)
    perm, cos_t, sin_t, ident, ones_row, ones_col, ones_lhs, tri = _host_consts()
    permC = np.concatenate([128 * h + perm for h in range(H)])
    permKV = np.concatenate([128 * g + perm for g in range(G)])
    wq_ = np.ascontiguousarray(gi("L0_attn_Wq")[:, permC])
    bq_ = np.ascontiguousarray(gi("L0_attn_bq")[permC][None, :])
    wkv_ = np.ascontiguousarray(np.concatenate(
        [gi("L0_attn_Wk")[:, permKV], gi("L0_attn_Wv")], axis=1))
    bkv_ = np.ascontiguousarray(np.concatenate(
        [gi("L0_attn_bk")[permKV], gi("L0_attn_bv")])[None, :])
    shared = {
        "wq": wq_, "bq": bq_, "wkv": wkv_, "bkv": bkv_,
        "wo": _pack_wo(gi("L0_attn_Wo")),
        "bo": np.ascontiguousarray(gi("L0_attn_bo")[None, :]),
        "alpha": np.ascontiguousarray(gi("L0_attn_alpha")[None, :]),
        "wr9": np.ascontiguousarray(np.concatenate([gi("L0_ffn_Wr"), gi("L0_ffn_Wsg")], axis=1)),
        "wgu": _pack_wgu(gi("L0_ffn_Wg"), gi("L0_ffn_Wu"),
                         gi("L0_ffn_Sg"), gi("L0_ffn_Su")),
        "wdd": _pack_wdd(gi("L0_ffn_Wd"), gi("L0_ffn_Sd")),
        "ident": ident, "ones_row": ones_row,
        "ones_col": ones_col, "ones_lhs": ones_lhs, "trimask": tri,
    }
    xt = x.reshape(NT, 128, C)
    ct = cos_t.reshape(NT, 128, 64)
    st = sin_t.reshape(NT, 128, 64)
    in_maps = []
    for pid in range(8):
        tiles = [8 * j + pid for j in range(NS)]
        h = pid % 2
        m = dict(shared)
        m["x_p1"] = np.ascontiguousarray(x[T // 2 * h:T // 2 * (h + 1)])
        m["cos_t"] = np.ascontiguousarray(cos_t[T // 2 * h:T // 2 * (h + 1)])
        m["sin_t"] = np.ascontiguousarray(sin_t[T // 2 * h:T // 2 * (h + 1)])
        m["x_own"] = np.ascontiguousarray(xt[tiles].reshape(NS * 128, C))
        m["cos_o"] = np.ascontiguousarray(ct[tiles].reshape(NS * 128, 64))
        m["sin_o"] = np.ascontiguousarray(st[tiles].reshape(NS * 128, 64))
        m["valid"] = _valid_table(pid)
        in_maps.append(m)
    return in_maps


def kernel(**inputs):
    in_maps = _prep_in_maps(inputs)
    nc, _ = build_nc(stage=5)
    res = run_bass_kernel_spmd(nc, in_maps, core_ids=list(range(8)))
    out = np.empty((T, C), dtype=np.float32)
    ot = out.reshape(NT, 128, C)
    for pid in range(8):
        o = np.asarray(res.results[pid]["out_own"]).reshape(NS, 128, C)
        for j in range(NS):
            ot[8 * j + pid] = o[j]
    return out.reshape(1, T, C)


# version 29
# speedup vs baseline: 1.0081x; 1.0081x over previous
import contextlib
import numpy as np
import ml_dtypes

import concourse.bass as bass
import concourse.mybir as mybir
from concourse.bass_utils import run_bass_kernel_spmd

F32 = mybir.dt.float32
BF16 = mybir.dt.bfloat16
AF = mybir.ActivationFunctionType
ALU = mybir.AluOpType
AX = mybir.AxisListType

T, C = 8192, 2048
NT = T // 128
NS = 8
H, G, HD = 16, 4, 128
KV = G * HD
E, F = 8, 1408
NF = F // 128
FS = 5632
EPS_LN = 1e-5
EPS_QK = 1e-6
KILL = -30000.0
TRI = -60000.0
NJS = [8 * j + 8 for j in range(NS)]
JSOFF = [0]
for _n in NJS:
    JSOFF.append(JSOFF[-1] + _n)
NVAL = JSOFF[-1]


class Sems:

    def __init__(self, nc, es):
        self.nc, self.es = nc, es
        self.h = {}
        self.n = {}

    def get(self, name):
        if name not in self.h:
            self.h[name] = self.es.enter_context(self.nc.semaphore(name))
            self.n[name] = 0
        return self.h[name]

    def inc(self, inst, name, k=1):
        inst.then_inc(self.get(name), k)
        self.n[name] += k
        return self.n[name]

    def wait(self, eng, name, val):
        eng.wait_ge(self.get(name), val)

    def selfsync(self, eng, inst, name):
        v = self.inc(inst, name, 1)
        eng.wait_ge(self.h[name], v)

    def snap(self, *names):
        return {nm: self.n.get(nm, 0) for nm in names}


def glob_js(sl, js):
    return JSOFF[sl] + js + 1


def build_nc(stage=5, debug=(), shared_kv=True):
    nc = bass.Bass("TRN2", target_bir_lowering=False, num_devices=8)

    def din(name, shape, dt=F32):
        return nc.dram_tensor(name, list(shape), dt, kind="ExternalInput").ap()

    x_p1 = din("x_p1", [T // 2, C])
    x_own = din("x_own", [NS * 128, C])
    wq = din("wq", [C, C])
    bq = din("bq", [1, C])
    wkv = din("wkv", [C, KV * 2])
    bkv = din("bkv", [1, KV * 2])
    wo = din("wo", [128, 4 * 16 * 512])
    bo = din("bo", [1, C])
    alpha = din("alpha", [1, H])
    wr9 = din("wr9", [C, 9])
    wgu_d = din("wgu", [E + 4, NF, 128, 16 * 256])
    wdd_d = din("wdd", [E + 4, 2, 128, NF * 1024])
    cos_t = din("cos_t", [T // 2, 64])
    sin_t = din("sin_t", [T // 2, 64])
    cos_o = din("cos_o", [NS * 128, 64])
    sin_o = din("sin_o", [NS * 128, 64])
    identd = din("ident", [128, 128], BF16)
    onesr_d = din("ones_row", [1, 512], BF16)
    onesc_d = din("ones_col", [128, 3], F32)
    oneslh_d = din("ones_lhs", [1, 128], F32)
    trimask_d = din("trimask", [128, 512], BF16)
    valid_d = din("valid", [128, NVAL])

    out_own = nc.dram_tensor("out_own", [NS * 128, C], F32, kind="ExternalOutput").ap()

    dbg = {}

    def dout(name, shape, dt=F32):
        dbg[name] = nc.dram_tensor("dbg_" + name, list(shape), dt, kind="ExternalOutput").ap()
        return dbg[name]

    _asp = "Shared" if shared_kv else "Local"
    kv_k = nc.dram_tensor("kv_k", [G, 128, T], BF16, kind="Internal",
                          addr_space=_asp).ap()
    kv_v = nc.dram_tensor("kv_v", [T, KV], BF16, kind="Internal",
                          addr_space=_asp).ap()
    kv_k_loc = nc.dram_tensor("kv_k_loc", [G, 128, T // 2], BF16, kind="Internal").ap()
    kv_v_loc = nc.dram_tensor("kv_v_loc", [T // 2, KV], BF16, kind="Internal").ap()
    NEXP = E + 4

    eso = contextlib.ExitStack()
    with eso:
        S = Sems(nc, eso)
        sb = lambda n_, sh, dt: eso.enter_context(nc.sbuf_tensor(n_, list(sh), dt))
        sbr = lambda n_, sh, dt: eso.enter_context(nc.sbuf_tensor(n_, list(sh), dt, side="right"))

        ident = sb("identb", [128, 128], BF16)

        onesc = sb("onescb", [128, 3], F32)

        alf = sb("alfb", [1, H], F32)
        gneg = sb("gnegb", [1, G], F32)

        esa = contextlib.ExitStack()
        sba = lambda n_, sh, dt: esa.enter_context(nc.sbuf_tensor(n_, list(sh), dt, side="right"))
        trim = sba("trimb", [128, 512], BF16)
        bo_sb = sba("bo_sbb", [1, C], BF16)
        onesr = sba("onesrb", [1, 512], BF16)
        oneslh = sba("oneslhb", [1, 128], F32)
        alpha_bc = sba("alpha_bcb", [128, H], F32)
        negb_bc = sba("negb_bcb", [128, G], F32)
        tmp30 = sba("tmp30b", [128, G], F32)
        bias_tab = sba("bias_tabb", [128, G, NVAL], F32)

        eskv = contextlib.ExitStack()
        wkv_sb = eskv.enter_context(nc.sbuf_tensor("wkv_sb", [128, 16, KV * 2], BF16))
        bq_sb = eskv.enter_context(nc.sbuf_tensor("bq_sbb", [1, C], BF16))
        bkv_sb = eskv.enter_context(nc.sbuf_tensor("bkv_sbb", [1, KV * 2], BF16))
        coso = eskv.enter_context(nc.sbuf_tensor("coso", [128, NS, 64], BF16))
        sino = eskv.enter_context(nc.sbuf_tensor("sino", [128, NS, 64], BF16))

        with nc.sbuf_tensor("validb", [128, NVAL], F32, side="right") as valid, \
             nc.psum_tensor("p0b", [128, H + G], F32) as p0b, \
             nc.Block("p0", no_gpsimd_drain=True) as blk:
            @blk.gpsimd
            def _(g):
                for dst, src in ((ident, identd), (onesr, onesr_d), (onesc, onesc_d),
                                 (oneslh, oneslh_d), (trim, trimask_d), (valid, valid_d),
                                 (alf, alpha), (bq_sb, bq), (bkv_sb, bkv), (bo_sb, bo)):
                    S.inc(g.dma_start(out=dst[...], in_=src[...]), "d_c0", 16)

            @blk.vector
            def _(v):
                S.wait(v, "d_c0", S.n["d_c0"])
                for g_ in range(G):
                    inst = v.reduce_max(gneg[:, g_:g_ + 1], alf[:, 4 * g_:4 * g_ + 4],
                                        axis=AX.X, apply_absolute_value=True, negate=True)
                S.inc(inst, "v_gn")
                S.wait(v, "t_p0", 1)
                v.tensor_copy(alpha_bc[:, :], p0b[:, 0:H])
                inst = v.tensor_copy(negb_bc[:, :], p0b[:, H:H + G])
                S.selfsync(v, inst, "v_ss")
                inst = v.tensor_scalar(tmp30[:, :], negb_bc[:, :], -KILL, None, ALU.add)
                S.selfsync(v, inst, "v_ss")
                for g_ in range(G):
                    inst = v.tensor_scalar(bias_tab[:, g_, :], valid[:, :],
                                           tmp30[:, g_:g_ + 1], KILL, ALU.mult, ALU.add)
                S.inc(inst, "v_p0")

            @blk.tensor
            def _(t):
                S.wait(t, "v_gn", 1)
                t.matmul(p0b[:, 0:H], oneslh[:, :], alf[:, :], start=True, stop=True)
                inst = t.matmul(p0b[:, H:H + G], oneslh[:, :], gneg[:, :], start=True, stop=True)
                S.inc(inst, "t_p0")

        if stage >= 1:
            with contextlib.ExitStack() as p1:
                psb = lambda n_, sh, dt: p1.enter_context(nc.psum_tensor(n_, list(sh), dt))
                p1b_ = lambda n_, sh, dt: p1.enter_context(nc.sbuf_tensor(n_, list(sh), dt))
                cosf = p1b_("cosf", [128, NT // 2, 64], BF16)
                sinf = p1b_("sinf", [128, NT // 2, 64], BF16)
                xin = p1b_("xin", [128, 4, C], F32)
                xa = p1b_("xa", [128, 4, C], BF16)
                xaT = p1b_("xaT", [128, 4, 16, 128], BF16)
                stats = p1b_("stats", [128, 24], F32)
                mv = p1b_("mv", [128, 4, 2], F32)
                lnb = p1b_("lnb", [128, 4], F32)
                rstd = p1b_("rstd", [128, 4], F32)
                ksb = p1b_("ksb", [128, 2, KV], F32)
                junk = p1b_("junk", [128, 512], F32)
                vjunk = p1b_("vjunk", [128, G, 256], F32)
                ssb = p1b_("ssb", [128, 2, G], F32)
                sqb = p1b_("sqb", [128, 2, G], F32)
                reck = p1b_("reck", [128, 2, G], F32)
                khat = p1b_("khat", [128, KV], F32)
                kr = p1b_("kr", [128, 4, KV], BF16)
                kTs = p1b_("kTs", [128, 3, G, 4, 128], BF16)
                vbuf = p1b_("vbuf", [128, 8, KV], BF16)
                trp = (psb("tr0", [128, 8, 128], BF16), psb("tr1", [128, 8, 128], BF16))
                kvp2b = psb("kvp2b", [128, 2, 512], F32)
                ktrp = [psb("ktr%d" % i_, [128, G, 128], BF16) for i_ in range(4)]

                NTH = NT // 2
                with nc.Block("p1", no_gpsimd_drain=True) as blk:
                    @blk.gpsimd
                    def _(g):
                        S.inc(g.dma_start(out=wkv_sb[...],
                                          in_=wkv.rearrange("(ct p) n -> p ct n", p=128)), "d_w1", 16)
                        S.inc(g.dma_start(out=cosf[...],
                                          in_=cos_t.rearrange("(n p) d -> p n d", p=128)), "d_w1", 16)
                        S.inc(g.dma_start(out=sinf[...],
                                          in_=sin_t.rearrange("(n p) d -> p n d", p=128)), "d_w1", 16)
                        S.inc(g.dma_start(out=coso[...],
                                          in_=cos_o.rearrange("(n p) d -> p n d", p=128)), "d_w2", 16)
                        S.inc(g.dma_start(out=sino[...],
                                          in_=sin_o.rearrange("(n p) d -> p n d", p=128)), "d_w2", 16)
                        vvr4 = kv_v_loc.rearrange("(n p) d -> p n d", p=128)
                        for w in range(NTH):
                            if w % 4 == 3:
                                strip = w // 4
                                S.wait(g, "s_ke", w + 1)
                                for g_ in range(G):
                                    S.inc(g.dma_start(
                                        out=kv_k_loc[g_][:, 512 * strip:512 * (strip + 1)],
                                        in_=kTs[:, strip % 3, g_, :, :].rearrange(
                                            "p a b -> p (a b)")),
                                          "d_ko%d" % (strip % 4), 16)
                            if w % 2 == 1:
                                S.wait(g, "s_ev", 2 * w + 2)
                                S.inc(g.dma_start(
                                    out=vvr4[:, w - 1:w + 1, :],
                                    in_=vbuf[:, (w - 1) % 8:(w - 1) % 8 + 2, :]),
                                      "d_vo%d" % ((w // 2) % 4), 16)

                    @blk.sync
                    def _(sy):
                        xf_t = x_p1.rearrange("(n p) c -> p n c", p=128)
                        for i in range(NTH):
                            if i >= 4:
                                S.wait(sy, "v_xa", i - 3)
                            S.inc(sy.dma_start(out=xin[:, i % 4, 0:1024],
                                               in_=xf_t[:, i, 0:1024]),
                                  "d_x%d" % (i % 4), 16)
                            S.inc(sy.dma_start(out=xin[:, i % 4, 1024:2048],
                                               in_=xf_t[:, i, 1024:2048]),
                                  "d_x%d" % (i % 4), 16)

                    @blk.vector
                    def _(v):
                        NW = NTH // 4
                        S.wait(v, "d_w1", 48)
                        for w in range(NW + 1):
                            if w < NW:
                                for j4 in range(4):
                                    i = 4 * w + j4
                                    S.wait(v, "d_x%d" % (i % 4), 32 * (i // 4 + 1))
                                    for cch in range(4):
                                        inst = v.bn_stats(stats[:, 6 * cch:6 * cch + 6],
                                                          xin[:, i % 4,
                                                              512 * cch:512 * cch + 512])
                                    S.selfsync(v, inst, "v_ss")
                                    inst = v.bn_aggr(mv[:, j4, :], stats[:, :])
                                    S.wait(v, "v_mv", S.inc(inst, "v_mv"))
                            if w >= 1:
                                if w >= 2:
                                    S.wait(v, "t_kt", 4 * (w - 1))
                                for j4 in range(4):
                                    j = 4 * (w - 1) + j4
                                    pj = j % 2
                                    S.wait(v, "s_rec", j + 1)
                                    for g_ in range(G):
                                        inst = v.tensor_scalar(
                                            khat[:, HD * g_:HD * (g_ + 1)],
                                            ksb[:, pj, HD * g_:HD * (g_ + 1)],
                                            reck[:, pj, g_:g_ + 1], None, ALU.mult)
                                    S.selfsync(v, inst, "v_ss")
                                    for g_ in range(G):
                                        x1 = khat[:, HD * g_:HD * g_ + 64]
                                        x2_ = khat[:, HD * g_ + 64:HD * g_ + 128]
                                        ct_ = cosf[:, j, :]
                                        st_ = sinf[:, j, :]
                                        v.tensor_mul(vjunk[:, g_, 0:64], x1, ct_)
                                        v.tensor_mul(vjunk[:, g_, 64:128], x2_, st_)
                                        v.tensor_mul(vjunk[:, g_, 128:192], x1, st_)
                                        inst = v.tensor_mul(vjunk[:, g_, 192:256], x2_, ct_)
                                    S.selfsync(v, inst, "v_ss")
                                    for g_ in range(G):
                                        v.tensor_sub(kr[:, j4, HD * g_:HD * g_ + 64],
                                                     vjunk[:, g_, 0:64], vjunk[:, g_, 64:128])
                                        inst = v.tensor_add(
                                            kr[:, j4, HD * g_ + 64:HD * g_ + 128],
                                            vjunk[:, g_, 128:192], vjunk[:, g_, 192:256])
                                    S.inc(inst, "v_kr")
                            if w < NW:
                                for j4 in range(4):
                                    i = 4 * w + j4
                                    S.wait(v, "s_rs", i + 1)
                                    inst = v.tensor_scalar(xa[:, i % 4, :], xin[:, i % 4, :],
                                                           mv[:, j4, 0:1], rstd[:, j4:j4 + 1],
                                                           ALU.subtract, ALU.mult)
                                    S.inc(inst, "v_xa")
                                if w >= 1:
                                    S.wait(v, "t_kv", 8 * w)
                                for j4 in range(4):
                                    i = 4 * w + j4
                                    for r in range(2):
                                        R = 2 * i + r
                                        S.wait(v, "t_tr", R + 1)
                                        inst = v.tensor_copy(xaT[:, i % 4, 8 * r:8 * r + 8, :],
                                                             trp[R % 2][:, :, :])
                                        S.inc(inst, "v_xt")

                    @blk.scalar
                    def _(s):
                        NW = NTH // 4
                        for w in range(NW + 2):
                            if w >= 2:
                                for j4 in range(4):
                                    k2 = 4 * (w - 2) + j4
                                    S.wait(s, "t_kt", k2 + 1)
                                    if k2 >= 12 and k2 % 4 == 0:
                                        st_i = k2 // 4 - 3
                                        S.wait(s, "d_ko%d" % (st_i % 4), 64 * (st_i // 4 + 1))
                                    inst = s.copy(kTs[:, (k2 // 4) % 3, :, k2 % 4, :],
                                                  ktrp[k2 % 4][:, :, :])
                                    S.inc(inst, "s_ke")
                            if 1 <= w <= NW:
                                for j4 in range(4):
                                    j = 4 * (w - 1) + j4
                                    pj = j % 2
                                    S.wait(s, "t_kv", 2 * j + 1)
                                    if j >= 2:
                                        S.wait(s, "v_kr", j - 1)
                                    inst = s.copy(ksb[:, pj, :], kvp2b[:, 0, :])
                                    S.wait(s, "s_ev", S.inc(inst, "s_ev"))
                                    for g_ in range(G):
                                        inst = s.activation(
                                            junk[:, HD * g_:HD * (g_ + 1)],
                                            ksb[:, pj, HD * g_:HD * (g_ + 1)],
                                            AF.Square, accum_out=ssb[:, pj, g_:g_ + 1])
                                    S.selfsync(s, inst, "s_ss")
                                    inst = s.activation(sqb[:, pj, :], ssb[:, pj, :], AF.Ln,
                                                        bias=onesc[:, 2:3], scale=1.0)
                                    S.selfsync(s, inst, "s_ss")
                                    inst = s.activation(reck[:, pj, :], sqb[:, pj, :], AF.Exp,
                                                        bias=0.0, scale=-0.5)
                                    S.inc(inst, "s_rec")
                                    S.wait(s, "t_kv", 2 * j + 2)
                                    if j >= 8:
                                        pr = ((j - 8) // 2)
                                        S.wait(s, "d_vo%d" % (pr % 4), 16 * (pr // 4 + 1))
                                    inst = s.copy(vbuf[:, j % 8, :], kvp2b[:, 1, :])
                                    S.inc(inst, "s_ev")
                            if w < NW:
                                for j4 in range(4):
                                    i = 4 * w + j4
                                    S.wait(s, "v_mv", i + 1)
                                    inst = s.activation(lnb[:, j4:j4 + 1], mv[:, j4, 1:2], AF.Ln,
                                                        bias=onesc[:, 1:2], scale=1.0)
                                    S.selfsync(s, inst, "s_ss")
                                    inst = s.activation(rstd[:, j4:j4 + 1], lnb[:, j4:j4 + 1],
                                                        AF.Exp, bias=0.0, scale=-0.5)
                                    S.inc(inst, "s_rs")
                    @blk.tensor
                    def _(t):
                        NW = NTH // 4
                        S.wait(t, "d_w1", 48)
                        S.wait(t, "v_p0", 1)
                        for w in range(NW + 1):
                            if 1 <= w <= NW:
                                for j4 in range(4):
                                    j = 4 * (w - 1) + j4
                                    S.wait(t, "v_xt", 2 * (j + 1))
                                    for q in range(2):
                                        c = 2 * j + q
                                        if c >= 2:
                                            S.wait(t, "s_ev", c - 1)
                                        for ct in range(16):
                                            t.matmul(kvp2b[:, q, :], xaT[:, j % 4, ct, :],
                                                     wkv_sb[:, ct, 512 * q:512 * (q + 1)],
                                                     start=(ct == 0), stop=False)
                                        inst = t.matmul(kvp2b[:, q, :], onesr[0:1, 0:128],
                                                        bkv_sb[:, 512 * q:512 * (q + 1)],
                                                        start=False, stop=True)
                                        S.inc(inst, "t_kv")
                            if w < NW:
                                for j4 in range(4):
                                    i = 4 * w + j4
                                    S.wait(t, "v_xa", i + 1)
                                    for r in range(2):
                                        R = 2 * i + r
                                        if R >= 2:
                                            S.wait(t, "v_xt", R - 1)
                                        for c8 in range(8):
                                            inst = t.transpose(
                                                trp[R % 2][:, c8, :],
                                                xa[:, i % 4,
                                                   128 * (8 * r + c8):128 * (8 * r + c8 + 1)],
                                                ident[:, :])
                                        S.inc(inst, "t_tr")
                            if w >= 1:
                                for j4 in range(4):
                                    k2 = 4 * (w - 1) + j4
                                    S.wait(t, "v_kr", k2 + 1)
                                    if k2 >= 4:
                                        S.wait(t, "s_ke", k2 - 3)
                                    for g_ in range(G):
                                        inst = t.transpose(
                                            ktrp[k2 % 4][:, g_, :],
                                            kr[:, k2 % 4, HD * g_:HD * (g_ + 1)],
                                            ident[:, :])
                                    S.inc(inst, "t_kt")

        p1_fin = S.snap("s_ke", "s_ev", "v_xt",
                        "d_ko0", "d_ko1", "d_ko2", "d_ko3",
                        "d_vo0", "d_vo1", "d_vo2", "d_vo3",
                        "t_kv", "t_kt", "v_kr", "v_xa", "t_tr", "d_cp")

        esq = contextlib.ExitStack()
        sbq = lambda n_, sh, dt: esq.enter_context(nc.sbuf_tensor(n_, list(sh), dt, side="right"))
        qT = sbq("qTb", [128, H, NS, 128], BF16)
        kT_own = sbq("kT_ownb", [128, G, NS, 128], BF16)
        v_own = sbq("v_ownb", [128, NS, G, 128], BF16)

        if stage >= 2:
            with contextlib.ExitStack() as p2s:
                lsb = lambda n_, sh, dt: p2s.enter_context(nc.sbuf_tensor(n_, list(sh), dt))
                xaT_own = lsb("xaT_own", [128, 16, NS, 128], BF16)
                wqb4 = lsb("wqb4", [128, 2, 16, 512], BF16)
                xin2 = lsb("xin2", [128, 2, C], F32)
                xa2 = lsb("xa2", [128, 2, C], BF16)
                stats2 = lsb("stats2", [128, 24], F32)
                mv2 = lsb("mv2", [128, 2, 2], F32)
                lnb2 = lsb("lnb2", [128, 2], F32)
                rstd2 = lsb("rstd2", [128, 2], F32)
                ksb2 = lsb("ksb2", [128, 2, KV], F32)
                vjunk2 = lsb("vjunk2", [128, G, 256], F32)
                ssb2 = lsb("ssb2", [128, 2, G], F32)
                sqb2 = lsb("sqb2", [128, 2, G], F32)
                reck2 = lsb("reck2", [128, 2, G], F32)
                kr2 = lsb("kr2", [128, 2, KV], BF16)
                qsb4 = lsb("qsb4", [128, 2, 512], F32)
                qss4 = lsb("qss4", [128, 2, 4], F32)
                qln4 = lsb("qln4", [128, 2, 4], F32)
                qrec4 = lsb("qrec4", [128, 2, 4], F32)
                qrm = lsb("qrm", [128, 4], F32)
                qr4 = lsb("qr4", [128, 2, 512], BF16)

                with contextlib.ExitStack() as pA:
                    psbA = lambda n_, sh, dt: pA.enter_context(nc.psum_tensor(n_, list(sh), dt))
                    junk2 = pA.enter_context(nc.sbuf_tensor("junk2", [128, 512], F32))
                    trp2 = (psbA("tr20", [128, 8, 128], BF16), psbA("tr21", [128, 8, 128], BF16))
                    kvp2 = (psbA("kv20", [128, KV * 2], F32), psbA("kv21", [128, KV * 2], F32))
                    ktrp2 = (psbA("ktr20", [128, G, 128], BF16), psbA("ktr21", [128, G, 128], BF16))

                    with nc.Block("p1bA", no_gpsimd_drain=True) as blk:
                        @blk.gpsimd
                        def _(g):
                            S.wait(g, "s_ke", p1_fin["s_ke"])
                            S.wait(g, "s_ev", p1_fin["s_ev"])
                            S.wait(g, "t_kt", p1_fin["t_kt"])
                            for nm_ in ("d_ko0", "d_ko1", "d_ko2", "d_ko3",
                                        "d_vo2", "d_vo3"):
                                S.wait(g, nm_, p1_fin[nm_])
                            S.wait(g, "d_vo0", p1_fin["d_vo0"])
                            S.wait(g, "d_vo1", p1_fin["d_vo1"])

                        @blk.sync
                        def _(sy):
                            for nm_ in ("d_ko0", "d_ko1", "d_ko2", "d_ko3",
                                        "d_vo0", "d_vo1", "d_vo2", "d_vo3"):
                                S.wait(sy, nm_, p1_fin[nm_])
                            pid_ = sy.partition_id()
                            hh_ = pid_ % 2
                            S.inc(sy.dma_start(
                                out=kv_k.rearrange("g p (hh s) -> p g hh s", hh=2)[
                                    :, :, bass.ts(hh_, 1), :],
                                in_=kv_k_loc.rearrange("g p s -> p g s")), "d_cp", 16)
                            S.inc(sy.dma_start(
                                out=kv_v.rearrange("(hh r p) d -> p hh r d", hh=2, p=128)[
                                    :, bass.ts(hh_, 1), :, :],
                                in_=kv_v_loc.rearrange("(r p) d -> p r d", p=128)),
                                  "d_cp", 16)
                            xo_t = x_own.rearrange("(n p) c -> p n c", p=128)
                            for sl in range(NS):
                                if sl >= 2:
                                    S.wait(sy, "v_xa2", sl - 1)
                                S.inc(sy.dma_start(out=xin2[:, sl % 2, 0:1024],
                                                   in_=xo_t[:, sl, 0:1024]),
                                      "d_x2%d" % (sl % 2), 16)
                                S.inc(sy.dma_start(out=xin2[:, sl % 2, 1024:2048],
                                                   in_=xo_t[:, sl, 1024:2048]),
                                      "d_x2%d" % (sl % 2), 16)

                        @blk.vector
                        def _(v):
                            S.wait(v, "d_w2", 32)
                            for it in range(NS // 2 + 1):
                                if it < NS // 2:
                                    for j2 in range(2):
                                        sl = 2 * it + j2
                                        p = sl % 2
                                        S.wait(v, "d_x2%d" % p, 32 * (sl // 2 + 1))
                                        for cch in range(4):
                                            inst = v.bn_stats(
                                                stats2[:, 6 * cch:6 * cch + 6],
                                                xin2[:, p, 512 * cch:512 * cch + 512])
                                        S.selfsync(v, inst, "v_ss")
                                        inst = v.bn_aggr(mv2[:, j2, :], stats2[:, :])
                                        S.wait(v, "v_mv2", S.inc(inst, "v_mv2"))
                                for j2 in range(2):
                                    j = 2 * (it - 1) + j2
                                    if not 0 <= j < NS:
                                        continue
                                    pj = j % 2
                                    S.wait(v, "s_rec2", j + 1)
                                    for g_ in range(G):
                                        inst = v.tensor_scalar(ksb2[:, pj, HD * g_:HD * (g_ + 1)],
                                                               ksb2[:, pj, HD * g_:HD * (g_ + 1)],
                                                               reck2[:, pj, g_:g_ + 1], None, ALU.mult)
                                    S.selfsync(v, inst, "v_ss")
                                    if j >= 2:
                                        S.wait(v, "t_kt2", j - 1)
                                    for g_ in range(G):
                                        x1 = ksb2[:, pj, HD * g_:HD * g_ + 64]
                                        x2_ = ksb2[:, pj, HD * g_ + 64:HD * g_ + 128]
                                        ct_ = coso[:, j, :]
                                        st_ = sino[:, j, :]
                                        v.tensor_mul(vjunk2[:, g_, 0:64], x1, ct_)
                                        v.tensor_mul(vjunk2[:, g_, 64:128], x2_, st_)
                                        v.tensor_mul(vjunk2[:, g_, 128:192], x1, st_)
                                        inst = v.tensor_mul(vjunk2[:, g_, 192:256], x2_, ct_)
                                    S.selfsync(v, inst, "v_ss")
                                    for g_ in range(G):
                                        v.tensor_sub(kr2[:, pj, HD * g_:HD * g_ + 64],
                                                     vjunk2[:, g_, 0:64], vjunk2[:, g_, 64:128])
                                        inst = v.tensor_add(kr2[:, pj, HD * g_ + 64:HD * g_ + 128],
                                                            vjunk2[:, g_, 128:192],
                                                            vjunk2[:, g_, 192:256])
                                    S.inc(inst, "v_kr2")
                                if it < NS // 2:
                                    for j2 in range(2):
                                        sl = 2 * it + j2
                                        p = sl % 2
                                        S.wait(v, "s_rs2", sl + 1)
                                        if j2 == 0 and it >= 1:
                                            S.wait(v, "t_tr2", 4 * it)
                                        inst = v.tensor_scalar(xa2[:, p, :], xin2[:, p, :],
                                                               mv2[:, j2, 0:1],
                                                               rstd2[:, j2:j2 + 1],
                                                               ALU.subtract, ALU.mult)
                                        S.inc(inst, "v_xa2")
                                    for j2 in range(2):
                                        sl = 2 * it + j2
                                        for r in range(2):
                                            R = 2 * sl + r
                                            S.wait(v, "t_tr2", R + 1)
                                            inst = v.tensor_copy(
                                                xaT_own[:, 8 * r:8 * r + 8, sl, :],
                                                trp2[R % 2][:, :, :])
                                            S.inc(inst, "v_xt2")

                        @blk.scalar
                        def _(s):
                            for it in range(NS // 2 + 2):
                                for j2 in range(2):
                                    j = 2 * (it - 1) + j2
                                    if not 0 <= j < NS:
                                        continue
                                    pj = j % 2
                                    S.wait(s, "t_kv2", j + 1)
                                    if j >= 1:
                                        S.wait(s, "v_kr2", j)
                                    inst = s.copy(ksb2[:, pj, :], kvp2[pj][:, 0:KV])
                                    S.wait(s, "s_kc2", S.inc(inst, "s_kc2"))
                                    for g_ in range(G):
                                        inst = s.activation(junk2[:, HD * g_:HD * (g_ + 1)],
                                                            ksb2[:, pj, HD * g_:HD * (g_ + 1)],
                                                            AF.Square,
                                                            accum_out=ssb2[:, pj, g_:g_ + 1])
                                    S.selfsync(s, inst, "s_ss")
                                    inst = s.activation(sqb2[:, pj, :], ssb2[:, pj, :], AF.Ln,
                                                        bias=onesc[:, 2:3], scale=1.0)
                                    S.selfsync(s, inst, "s_ss")
                                    inst = s.activation(reck2[:, pj, :], sqb2[:, pj, :], AF.Exp,
                                                        bias=0.0, scale=-0.5)
                                    S.inc(inst, "s_rec2")
                                    inst = s.copy(v_own[:, j, :, :], kvp2[pj][:, KV:2 * KV])
                                    S.inc(inst, "s_vc2")
                                if it < NS // 2:
                                    for j2 in range(2):
                                        sl = 2 * it + j2
                                        S.wait(s, "v_mv2", sl + 1)
                                        inst = s.activation(lnb2[:, j2:j2 + 1], mv2[:, j2, 1:2],
                                                            AF.Ln, bias=onesc[:, 1:2], scale=1.0)
                                        S.selfsync(s, inst, "s_ss")
                                        inst = s.activation(rstd2[:, j2:j2 + 1],
                                                            lnb2[:, j2:j2 + 1],
                                                            AF.Exp, bias=0.0, scale=-0.5)
                                        S.inc(inst, "s_rs2")
                                for j2 in range(2):
                                    k2 = 2 * (it - 2) + j2
                                    if not 0 <= k2 < NS:
                                        continue
                                    S.wait(s, "t_kt2", k2 + 1)
                                    inst = s.copy(kT_own[:, :, k2, :], ktrp2[k2 % 2][:, :, :])
                                    S.inc(inst, "s_ke2")

                        @blk.tensor
                        def _(t):
                            S.wait(t, "s_ev", p1_fin["s_ev"])
                            S.wait(t, "s_ke", p1_fin["s_ke"])
                            S.wait(t, "v_xt", p1_fin["v_xt"])
                            for it in range(NS // 2 + 2):
                                for j2 in range(2):
                                    j = 2 * (it - 1) + j2
                                    if not 0 <= j < NS:
                                        continue
                                    pj = j % 2
                                    S.wait(t, "v_xt2", 2 * (j + 1))
                                    if j >= 2:
                                        S.wait(t, "s_kc2", j - 1)
                                        S.wait(t, "s_vc2", j - 1)
                                    for ct in range(16):
                                        t.matmul(kvp2[pj][:, 0:512], xaT_own[:, ct, j, :],
                                                 wkv_sb[:, ct, 0:512], start=(ct == 0), stop=False)
                                        t.matmul(kvp2[pj][:, 512:1024], xaT_own[:, ct, j, :],
                                                 wkv_sb[:, ct, 512:1024], start=(ct == 0), stop=False)
                                    t.matmul(kvp2[pj][:, 0:512], onesr[0:1, 0:128], bkv_sb[:, 0:512],
                                             start=False, stop=True)
                                    inst = t.matmul(kvp2[pj][:, 512:1024], onesr[0:1, 0:128],
                                                    bkv_sb[:, 512:1024], start=False, stop=True)
                                    S.inc(inst, "t_kv2")
                                if it < NS // 2:
                                    for j2 in range(2):
                                        sl = 2 * it + j2
                                        p = sl % 2
                                        S.wait(t, "v_xa2", sl + 1)
                                        for r in range(2):
                                            R = 2 * sl + r
                                            if R >= 2:
                                                S.wait(t, "v_xt2", R - 1)
                                            for c8 in range(8):
                                                inst = t.transpose(
                                                    trp2[R % 2][:, c8, :],
                                                    xa2[:, p,
                                                        128 * (8 * r + c8):128 * (8 * r + c8 + 1)],
                                                    ident[:, :])
                                            S.inc(inst, "t_tr2")
                                for j2 in range(2):
                                    k2 = 2 * (it - 1) + j2
                                    if not 0 <= k2 < NS:
                                        continue
                                    pk = k2 % 2
                                    S.wait(t, "v_kr2", k2 + 1)
                                    if k2 >= 2:
                                        S.wait(t, "s_ke2", k2 - 1)
                                    for g_ in range(G):
                                        inst = t.transpose(ktrp2[pk][:, g_, :],
                                                           kr2[:, pk, HD * g_:HD * (g_ + 1)],
                                                           ident[:, :])
                                    S.inc(inst, "t_kt2")

                with contextlib.ExitStack() as pB:
                    psbB = lambda n_, sh, dt: pB.enter_context(nc.psum_tensor(n_, list(sh), dt))
                    qpp = (psbB("qp0", [128, 512], F32), psbB("qp1", [128, 512], F32))
                    qtrp = (psbB("qtr0", [128, 512], BF16), psbB("qtr1", [128, 512], BF16))
                    junkp = pB.enter_context(nc.sbuf_tensor("junkpb", [128, 512], BF16))

                    with nc.Block("p1bB", no_gpsimd_drain=True) as blk:
                        @blk.gpsimd
                        def _(g):
                            wqr = wq.rearrange("(ct p) n -> p ct n", p=128)
                            for q4 in range(4):
                                if q4 >= 2:
                                    S.wait(g, "t_q", NS * (q4 - 1))
                                S.inc(g.dma_start(out=wqb4[:, q4 % 2, :, :],
                                                  in_=wqr[:, :, 512 * q4:512 * (q4 + 1)]),
                                      "d_wq%d" % (q4 % 2), 16)
                            S.wait(g, "d_cp", S.n["d_cp"])

                        @blk.tensor
                        def _(t):
                            S.wait(t, "s_kc2", NS)
                            S.wait(t, "s_vc2", NS)
                            S.wait(t, "s_ke2", NS)
                            S.wait(t, "v_xt2", 2 * NS)
                            NU = 4 * NS
                            for it in range(NU + 2):
                                u = it
                                if u < NU:
                                    q4, sl = u // NS, u % NS
                                    S.wait(t, "d_wq%d" % (q4 % 2), 16 * (q4 // 2 + 1))
                                    if u >= 2:
                                        S.wait(t, "s_qc", u - 1)
                                    for ct in range(16):
                                        t.matmul(qpp[u % 2][:, :], xaT_own[:, ct, sl, :],
                                                 wqb4[:, q4 % 2, ct, :],
                                                 start=(ct == 0), stop=False)
                                    inst = t.matmul(qpp[u % 2][:, :], onesr[0:1, 0:128],
                                                    bq_sb[:, 512 * q4:512 * (q4 + 1)],
                                                    start=False, stop=True)
                                    S.inc(inst, "t_q")
                                b2 = it - 2
                                if b2 >= 0:
                                    S.wait(t, "v_qr", b2 + 1)
                                    if b2 >= 2:
                                        S.wait(t, "v_qe", b2 - 1)
                                    for dq in range(4):
                                        inst = t.transpose(
                                            qtrp[b2 % 2][:, 128 * dq:128 * (dq + 1)],
                                            qr4[:, b2 % 2, 128 * dq:128 * (dq + 1)],
                                            ident[:, :])
                                    S.inc(inst, "t_qt")

                        @blk.scalar
                        def _(s):
                            NU = 4 * NS
                            for it in range(NU + 1):
                                b2 = it - 1
                                if b2 >= 0:
                                    pb_ = b2 % 2
                                    S.wait(s, "t_q", b2 + 1)
                                    if b2 >= 2:
                                        S.wait(s, "v_qh", b2 - 1)
                                    inst = s.copy(qsb4[:, pb_, :], qpp[pb_][:, :])
                                    S.wait(s, "s_qc", S.inc(inst, "s_qc"))
                                    for dq in range(4):
                                        inst = s.activation(
                                            junkp[:, 128 * dq:128 * (dq + 1)],
                                            qsb4[:, pb_, 128 * dq:128 * (dq + 1)],
                                            AF.Square, accum_out=qss4[:, pb_, dq:dq + 1])
                                    S.selfsync(s, inst, "s_ss")
                                    inst = s.activation(qln4[:, pb_, :], qss4[:, pb_, :],
                                                        AF.Ln, bias=onesc[:, 2:3], scale=1.0)
                                    S.selfsync(s, inst, "s_ss")
                                    inst = s.activation(qrec4[:, pb_, :], qln4[:, pb_, :],
                                                        AF.Exp, bias=0.0, scale=-0.5)
                                    S.inc(inst, "s_qs")

                        @blk.vector
                        def _(v):
                            NU = 4 * NS
                            for it in range(NU + 2):
                                b2 = it - 1
                                if 0 <= b2 < NU:
                                    pb_ = b2 % 2
                                    q4, sl = b2 // NS, b2 % NS
                                    S.wait(v, "s_qs", b2 + 1)
                                    inst = v.tensor_mul(qrm[:, :], qrec4[:, pb_, :],
                                                        alpha_bc[:, 4 * q4:4 * q4 + 4])
                                    S.selfsync(v, inst, "v_ss")
                                    for dq in range(4):
                                        inst = v.tensor_scalar(
                                            qsb4[:, pb_, 128 * dq:128 * (dq + 1)],
                                            qsb4[:, pb_, 128 * dq:128 * (dq + 1)],
                                            qrm[:, dq:dq + 1], None, ALU.mult)
                                    S.selfsync(v, inst, "v_ss")
                                    ct_ = coso[:, sl, :]
                                    st_ = sino[:, sl, :]
                                    for dq in range(4):
                                        x1 = qsb4[:, pb_, 128 * dq:128 * dq + 64]
                                        x2_ = qsb4[:, pb_, 128 * dq + 64:128 * dq + 128]
                                        v.tensor_mul(vjunk2[:, dq, 0:64], x1, ct_)
                                        v.tensor_mul(vjunk2[:, dq, 64:128], x2_, st_)
                                        v.tensor_mul(vjunk2[:, dq, 128:192], x1, st_)
                                        inst = v.tensor_mul(vjunk2[:, dq, 192:256], x2_, ct_)
                                    S.wait(v, "v_qh", S.inc(inst, "v_qh"))
                                    if b2 >= 2:
                                        S.wait(v, "t_qt", b2 - 1)
                                    for dq in range(4):
                                        v.tensor_sub(qr4[:, pb_, 128 * dq:128 * dq + 64],
                                                     vjunk2[:, dq, 0:64],
                                                     vjunk2[:, dq, 64:128])
                                        inst = v.tensor_add(
                                            qr4[:, pb_, 128 * dq + 64:128 * (dq + 1)],
                                            vjunk2[:, dq, 128:192], vjunk2[:, dq, 192:256])
                                    S.inc(inst, "v_qr")
                                c = it - 2
                                if c >= 0:
                                    q4c, slc = c // NS, c % NS
                                    S.wait(v, "t_qt", c + 1)
                                    inst = v.tensor_copy(
                                        qT[:, 4 * q4c:4 * q4c + 4, slc, :],
                                        qtrp[c % 2].rearrange("p (h d) -> p h d", d=128))
                                    S.inc(inst, "v_qe")

        eskv.close()
        p1b_fin = S.snap("v_qe", "s_ke2", "s_vc2", "t_q", "d_wq0", "d_wq1", "s_qs", "v_ss")

        x2 = sb("x2b", [128, NS, C], BF16)
        eswo = contextlib.ExitStack()
        woA = eswo.enter_context(nc.sbuf_tensor("woA", [128, 2, 16, 512], BF16))

        if stage >= 3:
            if shared_kv:
                nc.all_core_barrier()
            def CNT(sl, js, g_):
                return 4 * (JSOFF[sl] + js) + g_ + 1

            strips = [(sl_, n_) for sl_ in range(NS) for n_ in range(sl_ + 1)]
            strip_idx = {sn: m for m, sn in enumerate(strips)}

            def strip_consumed_cnt(m):
                sl_, n_ = strips[m]
                last_js = min(8 * n_ + 8, NJS[sl_] - 1)
                return 4 * (JSOFF[sl_] + last_js) + 4

            with contextlib.ExitStack() as p2x:
                lsb = lambda n_, sh, dt: p2x.enter_context(nc.sbuf_tensor(n_, list(sh), dt))
                psb = lambda n_, sh, dt: p2x.enter_context(nc.psum_tensor(n_, list(sh), dt))
                kst = lsb("kst", [128, G, 2, 8, 128], BF16)
                vst = lsb("vst", [128, 2, 8, G, 128], BF16)
                expS = lsb("expS", [128, G, 2, 512], BF16)
                dacc = lsb("dacc", [128, G, 512], F32)
                recs = lsb("recs", [1, G, 512], F32)
                bcsb = lsb("bcsb", [128, G, 512], F32)
                scp = [psb("scp%d" % g_, [128, 512], F32) for g_ in range(G)]
                yp = [psb("yp%d" % g_, [128, 512], F32) for g_ in range(G)]

                with nc.Block("p2", no_gpsimd_drain=True) as blk:
                    @blk.gpsimd
                    def _(g):
                        wor = wo.rearrange("p (cc dt c) -> p cc dt c", dt=16, c=512)
                        for cc4 in range(2):
                            S.inc(g.dma_start(out=woA[:, cc4, :, :],
                                              in_=wor[:, cc4, :, :]), "d_wo%d" % cc4, 16)

                    @blk.sync
                    def _(sy):
                        S.wait(sy, "t_q", p1b_fin["t_q"])
                        kvr = kv_k.rearrange("g p s -> p g s")
                        vvr = kv_v.rearrange("(n p) d -> p n d", p=128)
                        for m, (sl_, n_) in enumerate(strips):
                            par = m % 2
                            if m >= 2:
                                S.wait(sy, "t_av", strip_consumed_cnt(m - 2))
                            S.inc(sy.dma_start(out=kst[:, :, par, :, :],
                                               in_=kvr[:, :, 1024 * n_:1024 * (n_ + 1)]),
                                  "d_ks%d" % par, 16)
                            S.inc(sy.dma_start(out=vst[:, par, :, :, :],
                                               in_=vvr[:, 8 * n_:8 * (n_ + 1), :]),
                                  "d_vs%d" % par, 16)

                    @blk.tensor
                    def _(t):
                        S.wait(t, "v_qe", p1b_fin["v_qe"])
                        S.wait(t, "s_ke2", p1b_fin["s_ke2"])
                        S.wait(t, "s_vc2", p1b_fin["s_vc2"])
                        for sl in range(NS):
                            for js in range(NJS[sl] + 1):
                                if js < NJS[sl]:
                                    for g_ in range(G):
                                        if js == 0:
                                            if sl >= 1:
                                                S.wait(t, "v_ye", 4 * (sl - 1) + g_ + 1)
                                            lhs = kT_own[:, g_, sl, :]
                                        else:
                                            S.wait(t, "s_ex", CNT(sl, js - 1, g_))
                                            m = strip_idx[(sl, (js - 1) // 8)]
                                            S.wait(t, "d_ks%d" % (m % 2), 16 * (m // 2 + 1))
                                            lhs = kst[:, g_, m % 2, (js - 1) % 8, :]
                                        inst = t.matmul(scp[g_][:, :], lhs,
                                                        qT[:, 4 * g_:4 * g_ + 4, sl, :],
                                                        start=True, stop=True)
                                        S.inc(inst, "t_sc")
                                if js >= 1:
                                    ja = js - 1
                                    for g_ in range(G):
                                        S.wait(t, "s_ex", CNT(sl, ja, g_))
                                        if ja == 0:
                                            lhs = v_own[:, sl, g_, :]
                                        else:
                                            m = strip_idx[(sl, (ja - 1) // 8)]
                                            S.wait(t, "d_vs%d" % (m % 2), 16 * (m // 2 + 1))
                                            lhs = vst[:, m % 2, (ja - 1) % 8, g_, :]
                                        inst = t.matmul(yp[g_][:, :], lhs,
                                                        expS[:, g_, ja % 2, :],
                                                        start=(ja == 0),
                                                        stop=(ja == NJS[sl] - 1))
                                        S.inc(inst, "t_av")
                            for g_ in range(G):
                                S.wait(t, "v_dn", CNT(sl, NJS[sl] - 1, g_))
                                S.wait(t, "s_ex", CNT(sl, NJS[sl] - 1, g_))
                                inst = t.matmul(scp[g_][0:1, :], onesc[:, 0:1], dacc[:, g_, :],
                                                start=True, stop=True)
                                S.inc(inst, "t_dr")
                            for g_ in range(G):
                                S.wait(t, "v_rc", 4 * sl + g_ + 1)
                                inst = t.matmul(scp[g_][:, :], oneslh[:, :], recs[0:1, g_, :],
                                                start=True, stop=True)
                                S.inc(inst, "t_bc")

                    @blk.scalar
                    def _(s):
                        for sl in range(NS):
                            for js in range(NJS[sl]):
                                for g_ in range(G):
                                    S.wait(s, "t_sc", CNT(sl, js, g_))
                                    if js == 0:
                                        S.wait(s, "v_tri", 4 * sl + g_ + 1)
                                    if js >= 2:
                                        S.wait(s, "t_av", CNT(sl, js - 2, g_))
                                        S.wait(s, "v_dn", CNT(sl, js - 2, g_))
                                    col = JSOFF[sl] + js
                                    inst = s.activation(expS[:, g_, js % 2, :], scp[g_][:, :],
                                                        AF.Exp,
                                                        bias=bias_tab[:, g_, col:col + 1],
                                                        scale=1.0)
                                    S.inc(inst, "s_ex")

                    @blk.vector
                    def _(v):
                        for sl in range(NS):
                            for js in range(NJS[sl]):
                                if js == 0:
                                    for g_ in range(G):
                                        S.wait(v, "t_sc", CNT(sl, 0, g_))
                                        inst = v.tensor_add(scp[g_][:, :], scp[g_][:, :],
                                                            trim[:, :])
                                        S.inc(inst, "v_tri")
                                for g_ in range(G):
                                    S.wait(v, "s_ex", CNT(sl, js, g_))
                                    if js == 0:
                                        inst = v.tensor_copy(dacc[:, g_, :], expS[:, g_, 0, :])
                                    else:
                                        S.wait(v, "v_dn", CNT(sl, js - 1, g_))
                                        inst = v.tensor_add(dacc[:, g_, :], dacc[:, g_, :],
                                                            expS[:, g_, js % 2, :])
                                    S.inc(inst, "v_dn")
                            for g_ in range(G):
                                S.wait(v, "t_dr", 4 * sl + g_ + 1)
                                inst = v.tensor_copy(recs[0:1, g_, :], scp[g_][0:1, :])
                                S.selfsync(v, inst, "v_ss")
                                inst = v.reciprocal(recs[0:1, g_, :], recs[0:1, g_, :])
                                S.selfsync(v, inst, "v_rc")
                            for g_ in range(G):
                                S.wait(v, "t_bc", 4 * sl + g_ + 1)
                                inst = v.tensor_copy(bcsb[:, g_, :], scp[g_][:, :])
                                S.selfsync(v, inst, "v_ss")
                                S.wait(v, "t_av", CNT(sl, NJS[sl] - 1, g_))
                                inst = v.tensor_mul(qT[:, 4 * g_:4 * g_ + 4, sl, :],
                                                    yp[g_][:, :].rearrange("p (h d) -> p h d", h=4),
                                                    bcsb[:, g_, :].rearrange("p (h d) -> p h d", h=4))
                                S.inc(inst, "v_ye")

        p2_fin = S.snap("v_ye", "t_bc", "s_ex", "v_dn", "v_rc", "d_ks0", "d_ks1",
                        "d_vs0", "d_vs1", "t_av", "t_sc")

        if stage >= 4:
            with contextlib.ExitStack() as p3x:
                lsb = lambda n_, sh, dt: p3x.enter_context(nc.sbuf_tensor(n_, list(sh), dt))
                psb = lambda n_, sh, dt: p3x.enter_context(nc.psum_tensor(n_, list(sh), dt))
                woB = lsb("woB", [128, 2, 16, 512], BF16)
                xr = lsb("xr", [128, 2, C], F32)
                usb = lsb("usb", [128, C], F32)
                stats3 = lsb("stats3", [128, 24], F32)
                mv3 = lsb("mv3", [128, 2], F32)
                lnb3 = lsb("lnb3", [128, 1], F32)
                rstd3 = lsb("rstd3", [128, 1], F32)
                op = (psb("op0", [128, C], F32), psb("op1", [128, C], F32))

                with nc.Block("p3", no_gpsimd_drain=True) as blk:
                    @blk.gpsimd
                    def _(g):
                        for nm in ("d_ks0", "d_ks1", "d_vs0", "d_vs1"):
                            S.wait(g, nm, p2_fin[nm])
                        wor = wo.rearrange("p (cc dt c) -> p cc dt c", dt=16, c=512)
                        for cc4 in range(2, 4):
                            S.inc(g.dma_start(out=woB[:, cc4 - 2, :, :],
                                              in_=wor[:, cc4, :, :]), "d_wo%d" % cc4, 16)

                    @blk.sync
                    def _(sy):
                        xo_t = x_own.rearrange("(n p) c -> p n c", p=128)
                        for sl in range(NS):
                            if sl >= 2:
                                S.wait(sy, "v_o3", sl - 1)
                            S.inc(sy.dma_start(out=xr[:, sl % 2, :], in_=xo_t[:, sl, :]),
                                  "d_x3%d" % (sl % 2), 16)

                    @blk.tensor
                    def _(t):
                        S.wait(t, "v_ye", p2_fin["v_ye"])
                        for sl in range(NS):
                            if sl >= 2:
                                S.wait(t, "v_o3", sl - 1)
                            for cc in range(4):
                                if sl == 0:
                                    S.wait(t, "d_wo%d" % cc, 16)
                                wocc = woA[:, cc, :, :] if cc < 2 else woB[:, cc - 2, :, :]
                                for h in range(H):
                                    t.matmul(op[sl % 2][:, 512 * cc:512 * (cc + 1)],
                                             qT[:, h, sl, :],
                                             wocc[:, h, :],
                                             start=(h == 0), stop=False)
                                inst = t.matmul(op[sl % 2][:, 512 * cc:512 * (cc + 1)],
                                                onesr[0:1, 0:128],
                                                bo_sb[:, 512 * cc:512 * (cc + 1)],
                                                start=False, stop=True)
                            S.inc(inst, "t_o3")

                    @blk.scalar
                    def _(s):
                        for sl in range(NS):
                            S.wait(s, "v_mv3", sl + 1)
                            inst = s.activation(lnb3[:, :], mv3[:, 1:2], AF.Ln,
                                                bias=onesc[:, 1:2], scale=1.0)
                            S.selfsync(s, inst, "s_ss")
                            inst = s.activation(rstd3[:, :], lnb3[:, :], AF.Exp,
                                                bias=0.0, scale=-0.5)
                            S.inc(inst, "s_rs3")

                    @blk.vector
                    def _(v):
                        for sl in range(NS):
                            p = sl % 2
                            S.wait(v, "t_o3", sl + 1)
                            for cch in range(4):
                                inst = v.bn_stats(stats3[:, 6 * cch:6 * cch + 6],
                                                  op[p][:, 512 * cch:512 * cch + 512])
                            S.selfsync(v, inst, "v_ss")
                            inst = v.bn_aggr(mv3[:, :], stats3[:, :])
                            S.inc(inst, "v_mv3")
                            S.wait(v, "s_rs3", sl + 1)
                            inst = v.tensor_scalar(usb[:, :], op[p][:, :], mv3[:, 0:1],
                                                   rstd3[:, 0:1], ALU.subtract, ALU.mult)
                            S.selfsync(v, inst, "v_ss")
                            S.wait(v, "d_x3%d" % p, 16 * (sl // 2 + 1))
                            inst = v.tensor_add(x2[:, sl, :], usb[:, :], xr[:, p, :])
                            S.inc(inst, "v_o3")

        p3_fin = S.snap("v_o3", "t_o3", "d_wo0", "d_wo1", "d_wo2", "d_wo3",
                        "d_x30", "d_x31")
        if stage >= 4:
            eswo.close()
            esq.close()
            esa.close()

        if stage >= 5:
            with contextlib.ExitStack() as p4x:
                lsb = lambda n_, sh, dt: p4x.enter_context(nc.sbuf_tensor(n_, list(sh), dt))
                xfT = lsb("xfT", [128, 16, NS * 128], BF16)
                acc = lsb("acc", [128, NS, C], BF16)
                C_sb = lsb("C_sb", [128, NS, 8], F32)
                gate = lsb("gate", [128, NS], F32)
                probs = lsb("probs", [128, 8], F32)
                esb = lsb("esb", [128, 8], F32)
                tju = lsb("tju", [128, 8], F32)
                negm = lsb("negm", [128, 1], F32)
                m1b = lsb("m1b", [128, 1], F32)
                m2b = lsb("m2b", [128, 1], F32)
                ssum = lsb("ssum", [128, 1], F32)
                mskb = lsb("mskb", [128, 8], F32)
                stats4 = lsb("stats4", [128, 24], F32)
                mv4 = lsb("mv4", [128, 2], F32)
                lnb4 = lsb("lnb4", [128, 1], F32)
                rstd4 = lsb("rstd4", [128, 1], F32)
                wr9_sb = lsb("wr9_sb", [128, 16, 9], BF16)
                TOK = NS * 128
                TCH = [(lo, min(lo + 512, TOK)) for lo in range(0, TOK, 512)]
                p4e = contextlib.ExitStack()
                lse = lambda n_, sh, dt: p4e.enter_context(nc.sbuf_tensor(n_, list(sh), dt))
                act = lse("act", [128, 2, NF, TOK], BF16)
                gsb = lse("gsb", [128, TOK], BF16)
                wgub = lse("wgub", [128, 2, 16, 256], BF16)
                wdb = lse("wdb", [128, 2, NF, 1024], BF16)

                with contextlib.ExitStack() as pp:
                    trF = (pp.enter_context(nc.psum_tensor("trF0", [128, 4, 128], BF16)),
                           pp.enter_context(nc.psum_tensor("trF1", [128, 4, 128], BF16)))
                    xfb = pp.enter_context(nc.sbuf_tensor("xfb", [128, 1, C], BF16))
                    r9 = pp.enter_context(nc.psum_tensor("r9", [128, 16], F32))
                    with nc.Block("p4a", no_gpsimd_drain=True) as blk:
                        @blk.gpsimd
                        def _(g):
                            for cc4 in range(4):
                                S.wait(g, "d_wo%d" % cc4, p3_fin["d_wo%d" % cc4])
                            S.wait(g, "d_x30", p3_fin["d_x30"])
                            S.wait(g, "d_x31", p3_fin["d_x31"])
                            S.inc(g.dma_start(out=wr9_sb[...],
                                              in_=wr9.rearrange("(ct p) n -> p ct n", p=128)),
                                  "d_w4", 16)
                            for hh in range(2):
                                S.inc(g.dma_start(
                                    out=wdb[:, hh, :, :],
                                    in_=wdd_d[0, hh].rearrange(
                                        "p (ft c) -> p ft c", c=1024)),
                                      "d_dw%d" % hh, 16)
                            for ft in range(2):
                                S.inc(g.dma_start(
                                    out=wgub[:, ft % 2, :, :],
                                    in_=wgu_d[0, ft].rearrange(
                                        "p (ct f) -> p ct f", f=256)),
                                      "d_gw%d" % (ft % 2), 32)

                        @blk.vector
                        def _(v):
                            for sl in range(NS):
                                p = sl % 2
                                for cch in range(4):
                                    inst = v.bn_stats(stats4[:, 6 * cch:6 * cch + 6],
                                                      x2[:, sl, 512 * cch:512 * cch + 512])
                                S.selfsync(v, inst, "v_ss")
                                inst = v.bn_aggr(mv4[:, :], stats4[:, :])
                                S.inc(inst, "v_mv4")
                                S.wait(v, "s_rs4", sl + 1)
                                if sl >= 1:
                                    S.wait(v, "t_tr4", 4 * sl)
                                inst = v.tensor_scalar(xfb[:, 0, :], x2[:, sl, :], mv4[:, 0:1],
                                                       rstd4[:, 0:1], ALU.subtract, ALU.mult)
                                S.inc(inst, "v_xf4")
                                for r in range(4):
                                    S.wait(v, "t_tr4", 4 * sl + r + 1)
                                    inst = v.tensor_copy(xfT[:, 4 * r:4 * r + 4,
                                                             128 * sl:128 * (sl + 1)],
                                                         trF[(4 * sl + r) % 2][:, :, :])
                                    S.inc(inst, "v_xt4")
                                S.wait(v, "t_r4", sl + 1)
                                inst = v.reduce_max(negm[:, :], r9[:, 0:8], axis=AX.X,
                                                    negate=True)
                                S.selfsync(v, inst, "v_ng4")
                                S.wait(v, "s_e4", sl + 1)
                                inst = v.reduce_sum(ssum[:, :], esb[:, :], axis=AX.X)
                                S.selfsync(v, inst, "v_ss")
                                inst = v.reciprocal(ssum[:, :], ssum[:, :])
                                S.selfsync(v, inst, "v_ss")
                                inst = v.tensor_scalar(probs[:, :], esb[:, :], ssum[:, 0:1],
                                                       None, ALU.mult)
                                S.selfsync(v, inst, "v_ss")
                                inst = v.reduce_max(m1b[:, :], probs[:, :], axis=AX.X)
                                S.selfsync(v, inst, "v_ss")
                                inst = v.tensor_scalar(mskb[:, :], probs[:, :], m1b[:, 0:1],
                                                       None, ALU.is_equal)
                                S.selfsync(v, inst, "v_ss")
                                inst = v.scalar_tensor_tensor(tju[:, :], mskb[:, :], -1e30,
                                                              probs[:, :], ALU.mult, ALU.add)
                                S.selfsync(v, inst, "v_ss")
                                inst = v.reduce_max(m2b[:, :], tju[:, :], axis=AX.X)
                                S.selfsync(v, inst, "v_ss")
                                inst = v.tensor_scalar(mskb[:, :], probs[:, :], m2b[:, 0:1],
                                                       None, ALU.is_ge)
                                S.selfsync(v, inst, "v_ss")
                                inst = v.tensor_mul(C_sb[:, sl, :], mskb[:, :], probs[:, :])
                                S.inc(inst, "v_c4")

                        @blk.scalar
                        def _(s):
                            for sl in range(NS):
                                S.wait(s, "v_mv4", sl + 1)
                                inst = s.activation(lnb4[:, :], mv4[:, 1:2], AF.Ln,
                                                    bias=onesc[:, 1:2], scale=1.0)
                                S.selfsync(s, inst, "s_ss")
                                inst = s.activation(rstd4[:, :], lnb4[:, :], AF.Exp,
                                                    bias=0.0, scale=-0.5)
                                S.inc(inst, "s_rs4")
                                S.wait(s, "v_ng4", sl + 1)
                                inst = s.activation(esb[:, :], r9[:, 0:8], AF.Exp,
                                                    bias=negm[:, 0:1], scale=1.0)
                                S.wait(s, "s_e4g", S.inc(inst, "s_e4g"))
                                inst = s.activation(gate[:, sl:sl + 1], r9[:, 8:9], AF.Sigmoid)
                                S.inc(inst, "s_e4")

                        @blk.tensor
                        def _(t):
                            S.wait(t, "v_o3", p3_fin["v_o3"])
                            S.wait(t, "d_w4", 16)
                            for sl in range(NS):
                                S.wait(t, "v_xf4", sl + 1)
                                for r in range(4):
                                    R = 4 * sl + r
                                    if R >= 2:
                                        S.wait(t, "v_xt4", R - 1)
                                    for c4 in range(4):
                                        inst = t.transpose(
                                            trF[R % 2][:, c4, :],
                                            xfb[:, 0,
                                                128 * (4 * r + c4):128 * (4 * r + c4 + 1)],
                                            ident[:, :])
                                    S.inc(inst, "t_tr4")
                                S.wait(t, "v_xt4", 4 * (sl + 1))
                                if sl >= 1:
                                    S.wait(t, "s_e4", sl)
                                for ct in range(16):
                                    inst = t.matmul(r9[:, 0:9],
                                                    xfT[:, ct, 128 * sl:128 * (sl + 1)],
                                                    wr9_sb[:, ct, :],
                                                    start=(ct == 0), stop=(ct == 15))
                                S.inc(inst, "t_r4")

                NEXP = E + 4
                with contextlib.ExitStack() as pe:
                    pg = pe.enter_context(nc.psum_tensor("pg", [128, TOK], F32))
                    pu = pe.enter_context(nc.psum_tensor("pu", [128, TOK], F32))
                    pd = (pe.enter_context(nc.psum_tensor("pd0", [128, 1024], F32)),
                          pe.enter_context(nc.psum_tensor("pd1", [128, 1024], F32)))

                    NCH = 2 * NS
                    with nc.Block("p4b", no_gpsimd_drain=True) as blk:
                        @blk.gpsimd
                        def _(g):
                            def wdload(e, hh):
                                if e >= 1:
                                    S.wait(g, "t_d", 2 * NS * (e - 1) + NS * (hh + 1))
                                S.inc(g.dma_start(
                                    out=wdb[:, hh, :, :],
                                    in_=wdd_d[e, hh].rearrange(
                                        "p (ft c) -> p ft c", c=1024)),
                                      "d_dw%d" % hh, 16)

                            for e in range(NEXP):
                                for ft in range(NF):
                                    gidx = e * NF + ft
                                    if gidx < 2:
                                        continue
                                    if gidx >= 2:
                                        S.wait(g, "t_gu", 2 * (gidx - 1))
                                    S.inc(g.dma_start(
                                        out=wgub[:, gidx % 2, :, :],
                                        in_=wgu_d[e, ft].rearrange(
                                            "p (ct f) -> p ct f", f=256)),
                                          "d_gw%d" % (gidx % 2), 32)
                                    if e >= 2 and ft == 1:
                                        wdload(e - 1, 1)
                                    if e >= 1 and ft == 7:
                                        wdload(e, 0)
                            wdload(NEXP - 1, 1)

                        @blk.tensor
                        def _(t):
                            def dchunk(ed, k):
                                cc, tt = k // NS, k % NS
                                didx = 2 * NS * ed + NS * cc + tt
                                par = didx % 2
                                if k == 0:
                                    S.wait(t, "v_ac4", NF * (ed + 1))
                                if tt == 0:
                                    S.wait(t, "d_dw%d" % cc, 16 * (ed + 1))
                                if didx >= 2:
                                    S.wait(t, "v_da4", didx - 1)
                                for ft in range(NF):
                                    t.matmul(pd[par][:, 0:512],
                                             act[:, ed % 2, ft, 128 * tt:128 * (tt + 1)],
                                             wdb[:, cc, ft, 0:512],
                                             start=(ft == 0), stop=(ft == NF - 1))
                                    inst = t.matmul(pd[par][:, 512:1024],
                                                    act[:, ed % 2, ft,
                                                        128 * tt:128 * (tt + 1)],
                                                    wdb[:, cc, ft, 512:1024],
                                                    start=(ft == 0), stop=(ft == NF - 1))
                                S.inc(inst, "t_d")

                            S.wait(t, "v_xt4", 4 * NS)
                            for e in range(NEXP):
                                dk = 0
                                for ft in range(NF):
                                    gidx = e * NF + ft
                                    S.wait(t, "d_gw%d" % (gidx % 2), 32 * (gidx // 2 + 1))
                                    S.wait(t, "s_si4", gidx)
                                    for ct in range(16):
                                        for lo, hi in TCH:
                                            inst = t.matmul(pg[:, lo:hi],
                                                            wgub[:, gidx % 2, ct, 0:128],
                                                            xfT[:, ct, lo:hi], start=(ct == 0),
                                                            stop=(ct == 15))
                                    S.inc(inst, "t_gu")
                                    if e >= 1 and dk < NCH:
                                        dchunk(e - 1, dk)
                                        dk += 1
                                    S.wait(t, "v_ac4", gidx)
                                    for ct in range(16):
                                        for lo, hi in TCH:
                                            inst = t.matmul(pu[:, lo:hi],
                                                            wgub[:, gidx % 2, ct, 128:256],
                                                            xfT[:, ct, lo:hi], start=(ct == 0),
                                                            stop=(ct == 15))
                                    S.inc(inst, "t_gu")
                                    if e >= 1 and ft < NCH - NF:
                                        dchunk(e - 1, dk)
                                        dk += 1
                                while e >= 1 and dk < NCH:
                                    dchunk(e - 1, dk)
                                    dk += 1
                            for k in range(NCH):
                                dchunk(NEXP - 1, k)

                        @blk.scalar
                        def _(s):
                            for e in range(NEXP):
                                for ft in range(NF):
                                    gidx = e * NF + ft
                                    S.wait(s, "t_gu", 2 * gidx + 1)
                                    S.wait(s, "v_ac4", gidx)
                                    inst = s.activation(gsb[:, :], pg[:, :], AF.Silu)
                                    S.inc(inst, "s_si4")

                        @blk.vector
                        def _(v):
                            for tt in range(NS):
                                inst = v.memset(acc[:, tt, :], 0.0)
                            S.inc(inst, "v_ai4")

                            def accupd(ed, k):
                                cc, tt = k // NS, k % NS
                                didx = 2 * NS * ed + NS * cc + tt
                                par = didx % 2
                                S.wait(v, "t_d", didx + 1)
                                if ed == 0:
                                    S.wait(v, "v_ai4", 1)
                                else:
                                    S.wait(v, "v_da4",
                                           2 * NS * (ed - 1) + NS * cc + tt + 1)
                                scal = (C_sb[:, tt, ed:ed + 1] if ed < E
                                        else gate[:, tt:tt + 1])
                                inst = v.scalar_tensor_tensor(
                                    acc[:, tt, 1024 * cc:1024 * (cc + 1)],
                                    pd[par][:, :], scal,
                                    acc[:, tt, 1024 * cc:1024 * (cc + 1)],
                                    ALU.mult, ALU.add)
                                S.inc(inst, "v_da4")

                            for e in range(NEXP):
                                dk = 0
                                for ft in range(NF):
                                    gidx = e * NF + ft
                                    if e >= 2 and ft == 0:
                                        S.wait(v, "t_d", 2 * NS * (e - 1))
                                    S.wait(v, "s_si4", gidx + 1)
                                    S.wait(v, "t_gu", 2 * gidx + 2)
                                    inst = v.tensor_mul(act[:, e % 2, ft, :],
                                                        pu[:, :], gsb[:, :])
                                    S.inc(inst, "v_ac4")
                                    if e >= 1 and dk < NCH:
                                        accupd(e - 1, dk)
                                        dk += 1
                                    if e >= 1 and ft < NCH - NF:
                                        accupd(e - 1, dk)
                                        dk += 1
                                while e >= 1 and dk < NCH:
                                    accupd(e - 1, dk)
                                    dk += 1
                            for k in range(NCH):
                                accupd(NEXP - 1, k)

                p4e.close()
                osb = lsb("osb", [128, 4, C], F32)
                p45 = S.snap("v_mv4", "s_rs4", "v_o3", "d_x30")
                with nc.Block("p5", no_gpsimd_drain=True) as blk:
                    @blk.vector
                    def _(v):
                        for sl in range(NS):
                            p = sl % 2
                            for cch in range(4):
                                inst = v.bn_stats(stats4[:, 6 * cch:6 * cch + 6],
                                                  acc[:, sl, 512 * cch:512 * cch + 512])
                            S.selfsync(v, inst, "v_ss")
                            inst = v.bn_aggr(mv4[:, :], stats4[:, :])
                            S.inc(inst, "v_mv4")
                            S.wait(v, "s_rs4", p45["s_rs4"] + sl + 1)
                            if sl >= 4:
                                S.wait(v, "d_x30", p45["d_x30"] + 32 * (sl - 3))
                            inst = v.tensor_scalar(osb[:, sl % 4, :], acc[:, sl, :],
                                                   mv4[:, 0:1],
                                                   rstd4[:, 0:1], ALU.subtract, ALU.mult)
                            S.selfsync(v, inst, "v_ss")
                            inst = v.tensor_add(osb[:, sl % 4, :], osb[:, sl % 4, :],
                                                x2[:, sl, :])
                            S.inc(inst, "v_o3")

                    @blk.scalar
                    def _(s):
                        for sl in range(NS):
                            S.wait(s, "v_mv4", p45["v_mv4"] + sl + 1)
                            inst = s.activation(lnb4[:, :], mv4[:, 1:2], AF.Ln,
                                                bias=onesc[:, 1:2], scale=1.0)
                            S.selfsync(s, inst, "s_ss")
                            inst = s.activation(rstd4[:, :], lnb4[:, :], AF.Exp,
                                                bias=0.0, scale=-0.5)
                            S.inc(inst, "s_rs4")

                    @blk.sync
                    def _(sy):
                        oo = out_own.rearrange("(n p) c -> p n c", p=128)
                        for sl in range(NS):
                            S.wait(sy, "v_o3", p45["v_o3"] + sl + 1)
                            S.inc(sy.dma_start(out=oo[:, sl, 0:1024],
                                               in_=osb[:, sl % 4, 0:1024]),
                                  "d_x30", 16)
                            S.inc(sy.dma_start(out=oo[:, sl, 1024:2048],
                                               in_=osb[:, sl % 4, 1024:2048]),
                                  "d_x30", 16)
                        S.wait(sy, "d_x30", S.n["d_x30"])

        if stage < 5:
            if "kv" in debug:
                dout("kv_k", [G, 128, T // 2], BF16)
                dout("kv_v", [T // 2, KV], BF16)
            if "qkv" in debug:
                dout("qT", [128, H, NS, 128], BF16)
                dout("kT_own", [128, G, NS, 128], BF16)
                dout("v_own", [128, NS, G, 128], BF16)
            with nc.Block("fin", no_gpsimd_drain=True) as blk:
                @blk.sync
                def _(sy):
                    waits = []
                    if stage >= 1:
                        waits += [("d_cp", p1_fin["d_cp"])]
                    if stage == 1:
                        waits += [("d_w2", 32)]
                    if stage >= 2 and stage < 3:
                        waits += [("v_qe", p1b_fin["v_qe"]), ("s_ke2", NS), ("s_vc2", NS)]
                    if stage >= 3:
                        waits += [("v_ye", p2_fin["v_ye"])]
                    if stage == 3:
                        waits += [("d_wo%d" % i_, 16) for i_ in range(4)]
                    for nm, val in waits:
                        S.wait(sy, nm, val)
                    if "kv" in debug:
                        S.inc(sy.dma_start(out=dbg["kv_k"][...], in_=kv_k_loc[...]), "d_db", 16)
                        S.inc(sy.dma_start(out=dbg["kv_v"][...], in_=kv_v_loc[...]), "d_db", 16)
                    if "qkv" in debug:
                        S.inc(sy.dma_start(out=dbg["qT"][...], in_=qT[...]), "d_db", 16)
                        S.inc(sy.dma_start(out=dbg["kT_own"][...], in_=kT_own[...]), "d_db", 16)
                        S.inc(sy.dma_start(out=dbg["v_own"][...], in_=v_own[...]), "d_db", 16)
                    if S.n.get("d_db", 0):
                        S.wait(sy, "d_db", S.n["d_db"])
        if stage < 4:
            eswo.close()
            esq.close()
            esa.close()
        if stage < 2:
            eskv.close()
    return nc, dbg


def _host_consts():
    perm = np.concatenate([np.arange(0, 128, 2), np.arange(1, 128, 2)])
    inv = 1.0 / (10000.0 ** (np.arange(0, HD, 2, dtype=np.float64) / HD))
    pos = np.arange(T, dtype=np.float64)
    fr = pos[:, None] * inv[None, :]
    cos_t = np.cos(fr).astype(np.float32)
    sin_t = np.sin(fr).astype(np.float32)
    ident = np.eye(128, dtype=ml_dtypes.bfloat16)
    ones_row = np.ones((1, 512), dtype=ml_dtypes.bfloat16)
    ones_col = np.concatenate([np.ones((128, 1), np.float32),
                               np.full((128, 1), EPS_LN, np.float32),
                               np.full((128, 1), 1e-12, np.float32)], axis=1)
    ones_lhs = np.ones((1, 128), dtype=np.float32)
    tri = np.zeros((128, 512), dtype=np.float32)
    t128 = np.where(np.arange(128)[:, None] <= np.arange(128)[None, :], 0.0, TRI)
    for h in range(4):
        tri[:, 128 * h:128 * (h + 1)] = t128
    return perm, cos_t, sin_t, ident, ones_row, ones_col, ones_lhs, tri.astype(ml_dtypes.bfloat16)


def _valid_table(pid):
    val = np.zeros((NVAL,), dtype=np.float32)
    idx = 0
    for j in range(NS):
        t = 8 * j + pid
        for js in range(NJS[j]):
            val[idx] = 1.0 if (js == 0 or (js - 1 < t)) else 0.0
            idx += 1
    return np.broadcast_to(val[None, :], (128, NVAL)).copy()


def _pack_wgu(wg, wu, sg, su):
    gs = np.concatenate([wg, sg.T.reshape(4, F, C).transpose(0, 2, 1)], axis=0)
    us = np.concatenate([wu, su.T.reshape(4, F, C).transpose(0, 2, 1)], axis=0)
    gsr = gs.reshape(12, 16, 128, NF, 128)
    usr = us.reshape(12, 16, 128, NF, 128)
    pk = np.stack([gsr, usr], axis=4)
    return np.ascontiguousarray(
        pk.transpose(0, 3, 2, 1, 4, 5).reshape(12, NF, 128, 16 * 256))


def _pack_wdd(wd, sd):
    ds = np.concatenate([wd, sd.reshape(4, F, C)], axis=0)
    dsr = ds.reshape(12, NF, 128, 2, 1024)
    return np.ascontiguousarray(
        dsr.transpose(0, 3, 2, 1, 4).reshape(12, 2, 128, NF * 1024))


def _pack_wo(wo):
    return np.ascontiguousarray(
        wo.reshape(16, 128, 4, 512).transpose(1, 2, 0, 3).reshape(128, -1))


def _prep_in_maps(inputs):
    gi = lambda k: np.ascontiguousarray(np.asarray(inputs[k], dtype=np.float32))
    x = gi("L0_attn_x").reshape(T, C)
    perm, cos_t, sin_t, ident, ones_row, ones_col, ones_lhs, tri = _host_consts()
    permC = np.concatenate([128 * h + perm for h in range(H)])
    permKV = np.concatenate([128 * g + perm for g in range(G)])
    wq_ = np.ascontiguousarray(gi("L0_attn_Wq")[:, permC])
    bq_ = np.ascontiguousarray(gi("L0_attn_bq")[permC][None, :])
    wkv_ = np.ascontiguousarray(np.concatenate(
        [gi("L0_attn_Wk")[:, permKV], gi("L0_attn_Wv")], axis=1))
    bkv_ = np.ascontiguousarray(np.concatenate(
        [gi("L0_attn_bk")[permKV], gi("L0_attn_bv")])[None, :])
    shared = {
        "wq": wq_, "bq": bq_, "wkv": wkv_, "bkv": bkv_,
        "wo": _pack_wo(gi("L0_attn_Wo")),
        "bo": np.ascontiguousarray(gi("L0_attn_bo")[None, :]),
        "alpha": np.ascontiguousarray(gi("L0_attn_alpha")[None, :]),
        "wr9": np.ascontiguousarray(np.concatenate([gi("L0_ffn_Wr"), gi("L0_ffn_Wsg")], axis=1)),
        "wgu": _pack_wgu(gi("L0_ffn_Wg"), gi("L0_ffn_Wu"),
                         gi("L0_ffn_Sg"), gi("L0_ffn_Su")),
        "wdd": _pack_wdd(gi("L0_ffn_Wd"), gi("L0_ffn_Sd")),
        "ident": ident, "ones_row": ones_row,
        "ones_col": ones_col, "ones_lhs": ones_lhs, "trimask": tri,
    }
    xt = x.reshape(NT, 128, C)
    ct = cos_t.reshape(NT, 128, 64)
    st = sin_t.reshape(NT, 128, 64)
    in_maps = []
    for pid in range(8):
        tiles = [8 * j + pid for j in range(NS)]
        h = pid % 2
        m = dict(shared)
        m["x_p1"] = np.ascontiguousarray(x[T // 2 * h:T // 2 * (h + 1)])
        m["cos_t"] = np.ascontiguousarray(cos_t[T // 2 * h:T // 2 * (h + 1)])
        m["sin_t"] = np.ascontiguousarray(sin_t[T // 2 * h:T // 2 * (h + 1)])
        m["x_own"] = np.ascontiguousarray(xt[tiles].reshape(NS * 128, C))
        m["cos_o"] = np.ascontiguousarray(ct[tiles].reshape(NS * 128, 64))
        m["sin_o"] = np.ascontiguousarray(st[tiles].reshape(NS * 128, 64))
        m["valid"] = _valid_table(pid)
        in_maps.append(m)
    return in_maps


def kernel(**inputs):
    in_maps = _prep_in_maps(inputs)
    nc, _ = build_nc(stage=5)
    res = run_bass_kernel_spmd(nc, in_maps, core_ids=list(range(8)))
    out = np.empty((T, C), dtype=np.float32)
    ot = out.reshape(NT, 128, C)
    for pid in range(8):
        o = np.asarray(res.results[pid]["out_own"]).reshape(NS, 128, C)
        for j in range(NS):
            ot[8 * j + pid] = o[j]
    return out.reshape(1, T, C)
